# version 31
# baseline (speedup 1.0000x reference)
import contextlib
import numpy as np
import ml_dtypes

import concourse.bass as bass
import concourse.mybir as mybir
from concourse.bass_utils import run_bass_kernel_spmd
from concourse.alu_op_type import AluOpType as OP
from concourse.bass_isa import ReduceOp
from concourse import library_config

F32 = mybir.dt.float32
BF16 = mybir.dt.bfloat16
I32 = mybir.dt.int32
AF = mybir.ActivationFunctionType
AXX = mybir.AxisListType.X
bfdt = ml_dtypes.bfloat16

D = 1024
H = 8
HD = 128
HD1 = HD + 1
FF = 4096
LR = 16
EPS = 1e-5


class Prog:

    ENGINES = ("sync", "gpsimd", "tensor", "vector", "scalar")

    def __init__(self):
        self.ops = {e: [] for e in self.ENGINES}
        self.counts = {}
        self.sems = {}
        self.last_wait = {e: {} for e in self.ENGINES}

    def em(self, eng, fn):
        self.ops[eng].append(fn)

    def op(self, eng, sem, build_fn, n=1):
        self.wait(eng, (sem, self.counts.get(sem, 0)))
        self.counts[sem] = self.counts.get(sem, 0) + n
        v = self.counts[sem]

        def f(e, _b=build_fn, _s=sem, _n=n):
            ins = _b(e)
            ins.then_inc(self.sems[_s], _n)
            return ins
        f._is_op = True
        self.em(eng, f)
        return (sem, v)

    def close(self, eng, sem):
        last = self.ops[eng].pop()
        assert not getattr(last, "_is_wait", False), "close() popped a wait"
        assert not getattr(last, "_is_op", False), "close() popped a tokened op"
        return self.op(eng, sem, last)

    def wait(self, eng, tok):
        if tok is None:
            return
        if isinstance(tok, (list, tuple)) and tok and isinstance(tok[0], tuple):
            for t in tok:
                self.wait(eng, t)
            return
        sem, v = tok
        if self.last_wait[eng].get(sem, -1) >= v:
            return
        self.last_wait[eng][sem] = v

        def f(e, _s=sem, _v=v):
            e.wait_ge(self.sems[_s], _v)
        f._is_wait = True
        self.em(eng, f)


def later(*toks):
    out = {}
    for t in toks:
        if t is None:
            continue
        if isinstance(t, list):
            for tt in t:
                if tt is not None:
                    s, v = tt
                    out[s] = max(out.get(s, -1), v)
            continue
        s, v = t
        out[s] = max(out.get(s, -1), v)
    return [(s, v) for s, v in out.items()]


def build_program(S=8192, debug=False):
    nc = bass.Bass("TRN2", target_bir_lowering=False)
    T1 = 512
    NT1 = S // T1
    CPT = T1 // 128
    NCH = S // 128
    T2 = 256
    NT2 = S // T2

    x_tm_d = nc.dram_tensor("x", [S, D], F32, kind="ExternalInput").ap()
    xT_d = nc.dram_tensor("xT", [D, S], F32, kind="ExternalInput").ap()
    wqT_d = nc.dram_tensor("wqT", [D, D], F32, kind="ExternalInput").ap()
    wkT_d = nc.dram_tensor("wkT", [D, D], F32, kind="ExternalInput").ap()
    wvT_d = nc.dram_tensor("wvT", [D, D], F32, kind="ExternalInput").ap()
    woT_d = nc.dram_tensor("woT", [D, D], F32, kind="ExternalInput").ap()
    nw_d = nc.dram_tensor("nw", [HD, 1], F32, kind="ExternalInput").ap()
    wfcT_d = nc.dram_tensor("wfcT", [D, FF], F32, kind="ExternalInput").ap()
    bfc_d = nc.dram_tensor("bfc", [128, FF // 128], F32, kind="ExternalInput").ap()
    afcN_d = nc.dram_tensor("afcN", [LR, D], F32, kind="ExternalInput").ap()
    bfcBT_d = nc.dram_tensor("bfcBT", [LR, FF], F32, kind="ExternalInput").ap()
    wprojT_d = nc.dram_tensor("wprojT", [FF, D], F32, kind="ExternalInput").ap()
    bproj_d = nc.dram_tensor("bproj", [1, D], F32, kind="ExternalInput").ap()
    aprojN_d = nc.dram_tensor("aprojN", [LR, FF], F32, kind="ExternalInput").ap()
    bprojBT_d = nc.dram_tensor("bprojBT", [LR, D], F32, kind="ExternalInput").ap()
    ident_d = nc.dram_tensor("ident", [128, 128], BF16, kind="ExternalInput").ap()
    identf_d = nc.dram_tensor("identf", [128, 128], F32, kind="ExternalInput").ap()
    mask4_d = nc.dram_tensor("mask4", [128, 512], BF16, kind="ExternalInput").ap()

    y_d = nc.dram_tensor("y", [S, D], F32, kind="ExternalOutput").ap()
    x1s_d = nc.dram_tensor("x1s", [S, D], F32, kind="Internal").ap()
    if debug:
        x1dbg_d = nc.dram_tensor("x1dbg", [S, D], F32, kind="ExternalOutput").ap()

    P = Prog()
    ctx = contextlib.ExitStack()
    for s in ["ld", "gd", "pe", "ve", "ac", "gp"]:
        P.sems[s] = ctx.enter_context(nc.semaphore(s))
        P.counts[s] = 0

    sb = lambda name, shape, dt: ctx.enter_context(nc.sbuf_tensor(name, shape, dt))
    ident16 = sb("ident16", [128, 128], BF16)
    mask4 = sb("mask4_s", [128, 512], BF16)
    identf = sb("identf_s", [128, 128], F32)
    ones_r_f = sb("ones_r_f", [1, 128], F32)
    g11 = sb("g11", [1, 1], F32)
    ones_c16 = sb("ones_c16", [128, 1], BF16)
    nw_col = sb("nw_col", [128, 1], F32)
    eps_col = sb("eps_col", [128, 1], F32)
    bfc_sb = sb("bfc_sb", [128, FF // 128], F32)
    bfcB2_sb = sb("bfcB2_sb", [LR, FF], BF16)
    bprojB2_sb = sb("bprojB2_sb", [33, D], BF16)
    aslc_sb = [sb(f"aslc{i}", [LR, 128], BF16) for i in range(2)]
    ones33 = sb("ones33", [33, 128], BF16)
    wmax_fc = sb("wmax_fc", [128, 1], F32)
    wmax_pj = sb("wmax_pj", [128, 1], F32)
    sv_fc = sb("sv_fc", [128, 1], F32)
    si_fc = sb("si_fc", [128, 1], F32)
    sv_pj = sb("sv_pj", [128, 1], F32)
    si_pj = sb("si_pj", [128, 1], F32)
    rows1 = sb("rows1", [128, 32], F32)

    R1 = lambda i, w=1: rows1[:, i:i + w]

    lstg_ctx = contextlib.ExitStack()
    lstg = lstg_ctx.enter_context(nc.sbuf_tensor("lstg", [LR, FF], F32))
    for dst, src in [(ident16, ident_d), (identf, identf_d), (mask4, mask4_d),
                     (nw_col, nw_d), (bfc_sb, bfc_d)]:
        t_last_const = P.op("sync", "ld",
                            lambda e, d=dst, s=src: e.dma_start(out=d[:], in_=s), 16)
    t_nw = ("ld", 16 * 4)
    t_bB1 = P.op("sync", "ld", lambda e: e.dma_start(out=lstg[:], in_=bfcBT_d), 16)
    P.wait("vector", t_bB1)
    t_bB1c = P.op("vector", "ve",
                  lambda e: e.tensor_scalar_mul(bfcB2_sb[:], lstg[:], 2.0))
    P.wait("sync", t_bB1c)
    t_bB2 = P.op("sync", "ld",
                 lambda e: e.dma_start(out=lstg[0:LR, 0:D], in_=bprojBT_d), 16)
    P.wait("vector", t_bB2)
    t_bB2c = P.op("vector", "ve",
                  lambda e: e.tensor_scalar_mul(bprojB2_sb[0:LR, :], lstg[0:LR, 0:D], 2.0))
    t_bprow = P.op("gpsimd", "gd", lambda e: e.dma_start(
        out=bprojB2_sb[32:33, :], in_=bproj_d), 16)

    ms = lambda ap, v: P.op("vector", "ve", lambda e, a=ap, vv=v: e.memset(a, vv))
    t_ms_toks = [ms(ones_c16[:], 1.0), ms(ones_r_f[:], 1.0), ms(eps_col[:], EPS),
                 ms(wmax_fc[:], 0.0), ms(wmax_pj[:], 0.0),
                 ms(ones33[:], 1.0)]

    lstg_ctx.close()

    with contextlib.ExitStack() as s1:
        sb1 = lambda name, shape, dt: s1.enter_context(nc.sbuf_tensor(name, shape, dt))
        WqT = sb1("WqT", [128, 8 * D], BF16)
        WkT = sb1("WkT", [128, 8 * D], BF16)
        WvT = sb1("WvT", [128, 8 * D], BF16)
        WoT = sb1("WoT", [128, 8 * D], BF16)
        x_tm = [sb1(f"x_tm{i}", [128, CPT * D], F32) for i in range(2)]
        x_fm = sb1("x_fm0", [128, 8 * T1], BF16)
        q_sb = sb1("q_sb", [128, H * T1], BF16)
        k_sb = sb1("k_sb", [128, H * T1], BF16)
        v_sb = sb1("v_sb", [128, CPT * D], BF16)
        ktm_sb = sb1("ktm_sb", [128, D], BF16)
        sTm_sb = sb1("sTm_sb", [128, D], BF16)
        state_sb = sb1("state_sb", [128, H * HD1], BF16)
        state_f32 = sb1("state_f32", [128, H * HD1], F32)
        o_stage = sb1("o_stg", [128, CPT * D], BF16)
        dn_stage = sb1("dn_stg", [128, 32], F32)
        o2f_sb = sb1("o2f_sb", [128, D], BF16)
        osq_sb = [sb1(f"osq{i}", [128, D], BF16) for i in range(2)]
        r1_sb = [sb1(f"r1_{i}", [128, D], F32) for i in range(4)]
        x1o_sb = [sb1(f"x1o_{i}", [128, D], F32) for i in range(2)]
        wst_sb = sb1("wst_sb", [128, 1024], F32)
        rowsA = sb1("rowsA", [128, 128], F32)
        wtmp = sb1("wtmp", [128, 1], F32)
        RA = lambda i, w=8: rowsA[:, i:i + w]

        ps1 = lambda name: s1.enter_context(nc.psum_tensor(name, [128, 512], F32))
        pR0, pR1 = ps1("pR0"), ps1("pR1")
        pSA, pSB = ps1("pSA"), ps1("pSB")
        pZ = ps1("pZ")
        pOA, pOB = ps1("pOA"), ps1("pOB")
        pD = ps1("pD")

        P.wait("gpsimd", t_bB1c)
        P.wait("gpsimd", t_bB2c)
        w_toks = []
        for Wbuf, Wd in [(WqT, wqT_d), (WkT, wkT_d), (WvT, wvT_d), (WoT, woT_d)]:
            w_toks.append(P.op("gpsimd", "gd", lambda e, b=Wbuf, d=Wd: e.dma_start(
                out=b[:].rearrange("p (c d) -> p c d", d=D),
                in_=d.rearrange("(c p) d -> c p d", p=128).transpose([1, 0, 2])), 16))
        P.wait("vector", w_toks[3])
        P.wait("vector", t_nw)
        t_wofold = P.op("vector", "ve",
                        lambda e: e.tensor_scalar_mul(WoT[:], WoT[:], nw_col[:]))
        t_st0 = ms(state_sb[:], 0.0)
        t_stf0 = ms(state_f32[:], 0.0)

        fr = {"pR0": None, "pR1": None, "pOA": None, "pOB": None, "pD": None,
              "pSA": None, "pSB": None, "pZ": None, "ostg": None,
              "o2f": None, "sTm": None,
              "scales": None,
              "ktm": None, "state": None, "osq": [None, None],
              "xtm": [None, None], "r1": [None] * 4, "x1o": [None, None],
              "wst": None, "o1rd": [None, None]}
        t_upd = None
        t_consts_tok = ("ld", 16 * 5)
        t_all_w = later(*w_toks, t_wofold, t_consts_tok, *t_ms_toks)
        P.wait("vector", t_consts_tok)

        def emit_wmax_job(job):
            wsrc, inner, acc = job
            P.wait("sync", fr["wst"])
            t_ws = P.op("sync", "ld", lambda e, s=wsrc:
                        e.dma_start(out=wst_sb[:], in_=s), 16)
            P.wait("vector", t_ws)
            P.op("vector", "ve", lambda e: e.reduce_max(
                wtmp[:], wst_sb[:], axis=AXX, apply_absolute_value=True))
            fr["wst"] = P.op("vector", "ve",
                             lambda e, a=acc: e.tensor_max(a[:], a[:], wtmp[:]))

        wmax_jobs = []
        for i in range(8):
            for hf in range(4):
                wmax_jobs.append((wfcT_d[i * 128:(i + 1) * 128,
                                         hf * 1024:(hf + 1) * 1024], None, wmax_fc))
        for i in range(32):
            wmax_jobs.append((wprojT_d[i * 128:(i + 1) * 128, :], None, wmax_pj))

        qkv_rot = [0]
        QBANKS = [("pR0", pR0), ("pR1", pR1), ("pD", pD)]

        def qkv_groups(t):
            jobs = []
            for W, dst in [(WqT, q_sb), (WkT, k_sb)]:
                for h in range(H):
                    def job(W=W, dst=dst, h_=h):
                        nm, pq = QBANKS[qkv_rot[0] % 3]
                        qkv_rot[0] += 1
                        P.wait("tensor", fr[nm])
                        for dc in range(8):
                            P.em("tensor", lambda e, d_=dc:
                                 e.matmul(pq[:],
                                          W[:, d_ * D + h_ * HD:d_ * D + h_ * HD + HD],
                                          x_fm[:, d_ * T1:(d_ + 1) * T1],
                                          start=(d_ == 0), stop=(d_ == 7)))
                        tk = P.close("tensor", "pe")
                        eng, sem = (("vector", "ve") if qkv_rot[0] % 2 == 0
                                    else ("scalar", "ac"))
                        P.wait(eng, tk)
                        if eng == "vector":
                            fr[nm] = P.op(eng, sem, lambda e: e.tensor_copy(
                                dst[:, h_ * T1:(h_ + 1) * T1], pq[:]))
                        else:
                            fr[nm] = P.op(eng, sem, lambda e: e.copy(
                                dst[:, h_ * T1:(h_ + 1) * T1], pq[:]))
                    jobs.append(job)
            for ic in range(CPT):
                for half in range(2):
                    def job(i_=ic, hf=half):
                        nm, pq = QBANKS[qkv_rot[0] % 3]
                        qkv_rot[0] += 1
                        P.wait("tensor", fr[nm])
                        for dc in range(8):
                            P.em("tensor", lambda e, d_=dc:
                                 e.matmul(pq[:],
                                          x_fm[:, d_ * T1 + i_ * 128:d_ * T1 + i_ * 128 + 128],
                                          WvT[:, d_ * D + hf * 512:d_ * D + hf * 512 + 512],
                                          start=(d_ == 0), stop=(d_ == 7)))
                        tk = P.close("tensor", "pe")
                        eng, sem = (("vector", "ve") if qkv_rot[0] % 2 == 0
                                    else ("scalar", "ac"))
                        P.wait(eng, tk)
                        if eng == "vector":
                            fr[nm] = P.op(eng, sem, lambda e: e.tensor_copy(
                                v_sb[:, i_ * D + hf * 512:i_ * D + hf * 512 + 512], pq[:]))
                        else:
                            fr[nm] = P.op(eng, sem, lambda e: e.copy(
                                v_sb[:, i_ * D + hf * 512:i_ * D + hf * 512 + 512], pq[:]))
                    jobs.append(job)
            return jobs

        def make_tail(t, xb, oev_toks, dn_toks, msq_gate):
            steps = []

            def s_sq(j):
                def f():
                    P.wait("scalar", oev_toks[j])
                    P.wait("scalar", fr["osq"][j % 2])
                    t_sq_ = P.op("scalar", "ac", lambda e: e.activation(
                        osq_sb[j % 2][:], o_stage[:, j * D:(j + 1) * D], AF.Square))
                    P.wait("vector", t_sq_)
                    msq_gate[j] = P.op("vector", "ve", lambda e: e.reduce_sum(
                        RA(64 + 8 * j)[:].rearrange("p (h o) -> p h o", o=1),
                        osq_sb[j % 2][:].rearrange("p (h x) -> p h x", x=HD),
                        axis=AXX))
                    fr["osq"][j % 2] = msq_gate[j]
                return f
            for j in range(CPT):
                steps.append(s_sq(j))

            def s_dn():
                P.wait("scalar", later(*dn_toks))
                t_dabs = P.op("scalar", "ac", lambda e: e.activation(
                    RA(0, 32)[:], dn_stage[:], AF.Abs))
                P.wait("vector", t_dabs)
                P.op("vector", "ve", lambda e: e.tensor_scalar_max(
                    RA(0, 32)[:], RA(0, 32)[:], 1.0))
                P.op("vector", "ve", lambda e: e.tensor_mul(
                    RA(32, 32)[:], RA(0, 32)[:], RA(0, 32)[:]))
                P.wait("vector", later(*msq_gate))
                t_inner = P.op("vector", "ve", lambda e: e.scalar_tensor_tensor(
                    RA(96, 32)[:], RA(32, 32)[:], float(HD) * EPS,
                    RA(64, 32)[:], OP.mult, OP.add))
                P.wait("scalar", t_inner)
                t_s = P.op("scalar", "ac", lambda e: e.activation(
                    RA(96, 32)[:], RA(96, 32)[:], AF.Sqrt, bias=0.0, scale=1.0 / HD))
                P.wait("vector", t_s)
                fr["rtot"] = P.op("vector", "ve", lambda e: e.reciprocal(
                    RA(96, 32)[:], RA(96, 32)[:]))
            steps.append(s_dn)

            def s_apply(j):
                def f():
                    P.wait("vector", fr["rtot"])
                    P.wait("vector", fr["osq"][j % 2])
                    P.wait("vector", fr["o1rd"][j % 2])
                    ap = P.op("vector", "ve", lambda e: e.tensor_tensor(
                        osq_sb[j % 2][:].rearrange("p (c f) -> p c f", f=128),
                        o_stage[:, j * D:(j + 1) * D]
                        .rearrange("p (c f) -> p c f", f=128),
                        RA(96 + 8 * j)[:].rearrange("p (c o) -> p c o", o=1)
                        .broadcast_to([128, 8, 128]), OP.mult))
                    fr["apl", j] = ap
                return f

            def s_wo(j):
                def f():
                    P.wait("tensor", fr["apl", j])
                    P.wait("tensor", fr["o2f"])
                    P.wait("tensor", fr["pZ"])
                    for hh in range(4):
                        P.em("tensor", lambda e, h_=hh, c4=hh:
                             e.matmul(pZ[:, c4 * 128:c4 * 128 + 128],
                                      osq_sb[j % 2][:, h_ * 128:h_ * 128 + 128],
                                      ident16[:], start=True, stop=True))
                    tk0 = P.close("tensor", "pe")
                    P.wait("tensor", fr["pSA"])
                    for hh in range(4):
                        P.em("tensor", lambda e, h_=4 + hh, c4=hh:
                             e.matmul(pSA[:, c4 * 128:c4 * 128 + 128],
                                      osq_sb[j % 2][:, h_ * 128:h_ * 128 + 128],
                                      ident16[:], start=True, stop=True))
                    tk1 = P.close("tensor", "pe")
                    P.wait("scalar", tk0)
                    fr["pZ"] = P.op("scalar", "ac", lambda e: e.copy(
                        o2f_sb[:, 0:512], pZ[:]))
                    P.wait("vector", tk1)
                    fr["pSA"] = P.op("vector", "ve", lambda e: e.tensor_copy(
                        o2f_sb[:, 512:1024], pSA[:]))
                    fr["o1rd"][j % 2] = ("pe", P.counts["pe"])
                    P.wait("tensor", fr["pZ"])
                    P.wait("tensor", fr["pOA"])
                    P.wait("tensor", fr["pOB"])
                    for half, po in [(0, pOA), (1, pOB)]:
                        for h in range(4):
                            P.em("tensor", lambda e, p=po, h_=h, hf=half:
                                 e.matmul(p[:], o2f_sb[:, h_ * 128:h_ * 128 + 128],
                                          WoT[:, h_ * D + hf * 512:h_ * D + hf * 512 + 512],
                                          start=(h_ == 0), stop=False))
                    P.wait("tensor", fr["pSA"])
                    for half, po in [(0, pOA), (1, pOB)]:
                        for h in range(4, H):
                            P.em("tensor", lambda e, p=po, h_=h, hf=half:
                                 e.matmul(p[:], o2f_sb[:, h_ * 128:h_ * 128 + 128],
                                          WoT[:, h_ * D + hf * 512:h_ * D + hf * 512 + 512],
                                          start=False, stop=(h_ == 7)))
                    t_wo = P.close("tensor", "pe")
                    fr["o2f"] = t_wo
                    P.wait("vector", t_wo)
                    P.wait("vector", fr["r1"][j])
                    ra = P.op("vector", "ve", lambda e: e.scalar_tensor_tensor(
                        r1_sb[j][:, 0:512], pOA[:], 1.0,
                        x_tm[xb][:, j * D:j * D + 512],
                        OP.mult, OP.add, accum_out=R1(2 * j)[:]))
                    rbt = P.op("vector", "ve", lambda e: e.scalar_tensor_tensor(
                        r1_sb[j][:, 512:1024], pOB[:], 1.0,
                        x_tm[xb][:, j * D + 512:j * D + 1024],
                        OP.mult, OP.add, accum_out=R1(2 * j + 1)[:]))
                    fr["pOA"] = later(ra)
                    fr["pOB"] = later(rbt)
                    fr["res", j] = later(ra, rbt)
                    if j == CPT - 1:
                        fr["xtm"][xb] = rbt
                return f
            for j in range(CPT):
                steps.append(s_apply(j))
                steps.append(s_wo(j))

            def s_ssq(j):
                def f():
                    P.wait("scalar", fr["res", j])
                    P.wait("scalar", fr["apl", j])
                    P.wait("scalar", fr["ostg2", j] if ("ostg2", j) in fr else None)
                    tk = P.op("scalar", "ac", lambda e: e.activation(
                        o_stage[:, j * D:(j + 1) * D], r1_sb[j][:], AF.Square,
                        accum_out=R1(28 + j)[:]))
                    fr["ssq", j] = tk
                    fr["ostg2", j] = tk
                return f
            for j in range(CPT):
                steps.append(s_ssq(j))

            def s_ln():
                P.wait("vector", later(*[fr["ssq", j] for j in range(CPT)]))
                P.op("vector", "ve", lambda e: e.tensor_add(
                    R1(8, 4)[:], R1(0, 8)[:].rearrange("p (c two) -> p c two", two=2)
                    [:, :, 0:1].squeeze(2) if False else R1(0, 8)[:]
                    .rearrange("p (c two) -> p c two", two=2)[:, :, 0],
                    R1(0, 8)[:].rearrange("p (c two) -> p c two", two=2)[:, :, 1]))
                P.op("vector", "ve", lambda e: e.tensor_scalar_mul(
                    R1(8, 4)[:], R1(8, 4)[:], 1.0 / D))
                P.op("vector", "ve", lambda e: e.tensor_mul(
                    R1(12, 4)[:], R1(8, 4)[:], R1(8, 4)[:]))
                t_var = P.op("vector", "ve", lambda e: e.scalar_tensor_tensor(
                    R1(16, 4)[:], R1(28, 4)[:], 1.0 / D, R1(12, 4)[:],
                    OP.mult, OP.subtract))
                P.wait("scalar", t_var)
                t_sq = P.op("scalar", "ac", lambda e: e.activation(
                    R1(20, 4)[:], R1(16, 4)[:], AF.Sqrt, bias=eps_col[:], scale=1.0))
                P.wait("vector", t_sq)
                P.op("vector", "ve", lambda e: e.reciprocal(R1(20, 4)[:], R1(20, 4)[:]))
                fr["b1"] = P.op("vector", "ve", lambda e: e.scalar_tensor_tensor(
                    R1(24, 4)[:], R1(8, 4)[:], -1.0, R1(20, 4)[:],
                    OP.mult, OP.mult))
            late = [s_ln]

            def s_out(j):
                def f():
                    c = t * CPT + j
                    P.wait("scalar", fr["b1"])
                    P.wait("scalar", fr["x1o"][j % 2])
                    t_x1 = P.op("scalar", "ac", lambda e: e.activation(
                        x1o_sb[j % 2][:], r1_sb[j][:], AF.Identity,
                        bias=R1(24 + j)[:], scale=R1(20 + j)[:]))
                    fr["r1"][j] = t_x1
                    P.wait("gpsimd", t_x1)
                    t_st = P.op("gpsimd", "gd", lambda e: e.dma_start(
                        out=x1s_d[c * 128:(c + 1) * 128, :], in_=x1o_sb[j % 2][:]), 16)
                    if debug:
                        t_st = P.op("gpsimd", "gd", lambda e: e.dma_start(
                            out=x1dbg_d[c * 128:(c + 1) * 128, :],
                            in_=x1o_sb[j % 2][:]), 16)
                    fr["x1o"][j % 2] = t_st
                return f
            for j in range(CPT):
                late.append(s_out(j))
            return steps, late

        scales_done = [False]

        def emit_scales():
            scales_done[0] = True
            for wmax, sv, si in [(wmax_fc, sv_fc, si_fc), (wmax_pj, sv_pj, si_pj)]:
                P.wait("tensor", fr["wst"])
                P.wait("tensor", later(*t_ms_toks))
                P.wait("tensor", fr["pR0"])
                P.em("tensor", lambda e, w=wmax: e.matmul(
                    pR0[0:1, 0:128], w[:], identf[:], start=True, stop=True))
                tk = P.close("tensor", "pe")
                P.wait("vector", tk)
                t_g = P.op("vector", "ve", lambda e: e.reduce_max(
                    g11[:], pR0[0:1, 0:128], axis=AXX))
                P.wait("tensor", t_g)
                P.em("tensor", lambda e: e.matmul(pR0[:, 0:1], ones_r_f[:], g11[:],
                                                  start=True, stop=True))
                tk = P.close("tensor", "pe")
                P.wait("vector", tk)
                P.op("vector", "ve", lambda e, s=sv: e.tensor_scalar_mul(
                    s[:], pR0[:, 0:1], 1.0 / 127.0))
                P.op("vector", "ve", lambda e, s=si: e.reciprocal(s[:], pR0[:, 0:1]))
                t_si = P.op("vector", "ve", lambda e, s=si: e.tensor_scalar_mul(
                    s[:], s[:], 127.0))
                fr["pR0"] = t_si
                fr["wst"] = t_si
            fr["scales"] = ("ve", P.counts["ve"])

        tail_steps = []
        late_steps = []
        t_xf = None
        for t in range(NT1):
            xb = t % 2
            P.wait("sync", fr["xtm"][xb])
            t_x = P.op("sync", "ld", lambda e, tt=t, b=xb: e.dma_start(
                out=x_tm[b][:].rearrange("p (c d) -> p c d", d=D),
                in_=x_tm_d[tt * T1:(tt + 1) * T1, :]
                    .rearrange("(c p) d -> c p d", p=128).transpose([1, 0, 2])), 16)
            if t == 0:
                t_xf = P.op("gpsimd", "gd", lambda e: e.dma_start(
                    out=x_fm[:].rearrange("p (c i) -> p c i", i=T1),
                    in_=xT_d[:, 0:T1]
                        .rearrange("(c p) i -> c p i", p=128).transpose([1, 0, 2])), 16)
                P.wait("tensor", t_all_w)

            P.wait("tensor", t_xf)
            for gi, grp in enumerate(qkv_groups(t)):
                grp()
                if gi >= 4 and tail_steps:
                    tail_steps.pop(0)()
            while tail_steps:
                tail_steps.pop(0)()
            if t + 1 < NT1:
                P.wait("gpsimd", ("pe", P.counts["pe"]))
                t_xf = P.op("gpsimd", "gd", lambda e, tt=t + 1: e.dma_start(
                    out=x_fm[:].rearrange("p (c i) -> p c i", i=T1),
                    in_=xT_d[:, tt * T1:(tt + 1) * T1]
                        .rearrange("(c p) i -> c p i", p=128).transpose([1, 0, 2])), 16)

            oev_toks = [None] * CPT
            dn_toks = [None] * CPT
            msq_gate = [None] * CPT
            for ic in range(CPT):
                c = t * CPT + ic
                P.wait("tensor", fr["pR0"])
                P.wait("tensor", fr["pR1"])
                P.wait("tensor", fr["ktm"])
                for h in range(H):
                    pq = pR0 if h < 4 else pR1
                    P.em("tensor", lambda e, p=pq, h_=h, i_=ic:
                         e.matmul(p[:, (h_ % 4) * 128:(h_ % 4) * 128 + 128],
                                  k_sb[:, h_ * T1 + i_ * 128:h_ * T1 + i_ * 128 + 128],
                                  ident16[:], start=True, stop=True))
                t_ktm = P.close("tensor", "pe")
                P.wait("tensor", fr["pSA"])
                P.wait("tensor", fr["pSB"])
                for h in range(H):
                    pq = pSA if h < 4 else pSB
                    P.em("tensor", lambda e, p=pq, h_=h, i_=ic:
                         e.matmul(p[:, (h_ % 4) * 128:(h_ % 4) * 128 + 128],
                                  k_sb[:, h_ * T1 + i_ * 128:h_ * T1 + i_ * 128 + 128],
                                  q_sb[:, h_ * T1 + i_ * 128:h_ * T1 + i_ * 128 + 128],
                                  start=True, stop=True))
                t_sc = P.close("tensor", "pe")
                P.wait("scalar", t_ktm)
                ev0 = P.op("scalar", "ac", lambda e:
                           e.copy(ktm_sb[:, 0:512], pR0[:]))
                ev1 = P.op("scalar", "ac", lambda e:
                           e.copy(ktm_sb[:, 512:1024], pR1[:]))
                P.wait("vector", t_sc)
                P.wait("vector", fr["sTm"])
                m0 = P.op("vector", "ve", lambda e: e.tensor_tensor(
                    sTm_sb[:, 0:512], pSA[:], mask4[:], OP.mult))
                m1 = P.op("vector", "ve", lambda e: e.tensor_tensor(
                    sTm_sb[:, 512:1024], pSB[:], mask4[:], OP.mult))
                fr["pSA"], fr["pSB"] = m0, m1
                if late_steps:
                    late_steps.pop(0)()
                if c > 0:
                    P.wait("vector", fr["state"])
                    t_scpy = P.op("vector", "ve", lambda e: e.tensor_copy(
                        state_sb[:], state_f32[:]))
                else:
                    t_scpy = later(t_st0, t_stf0)
                P.wait("tensor", ev0)
                P.wait("tensor", ev1)
                P.wait("tensor", fr["pZ"])
                for h in range(H):
                    psb = pR0 if h < 4 else pR1
                    P.em("tensor", lambda e, p=psb, h_=h, i_=ic:
                         e.matmul(p[:, (h_ % 4) * 128:(h_ % 4) * 128 + 128],
                                  ktm_sb[:, h_ * 128:h_ * 128 + 128],
                                  v_sb[:, i_ * D + h_ * 128:i_ * D + h_ * 128 + 128],
                                  start=(h_ % 4 == 0), stop=(h_ % 4 == 3)))
                for h in range(H):
                    P.em("tensor", lambda e, h_=h:
                         e.matmul(pZ[:, h_:h_ + 1],
                                  ktm_sb[:, h_ * 128:h_ * 128 + 128], ones_c16[:],
                                  start=(h_ == 0), stop=(h_ == 7)))
                t_upd = P.close("tensor", "pe")
                P.wait("vector", t_upd)
                P.op("vector", "ve", lambda e: e.tensor_tensor(
                    state_f32[:].rearrange("p (h x) -> p h x", x=HD1)[:, 0:4, 0:HD],
                    state_f32[:].rearrange("p (h x) -> p h x", x=HD1)[:, 0:4, 0:HD],
                    pR0[:].rearrange("p (h x) -> p h x", x=HD), OP.add))
                P.op("vector", "ve", lambda e: e.tensor_tensor(
                    state_f32[:].rearrange("p (h x) -> p h x", x=HD1)[:, 4:8, 0:HD],
                    state_f32[:].rearrange("p (h x) -> p h x", x=HD1)[:, 4:8, 0:HD],
                    pR1[:].rearrange("p (h x) -> p h x", x=HD), OP.add))
                t_sadd = P.op("vector", "ve", lambda e: e.tensor_tensor(
                    state_f32[:, HD::HD1], state_f32[:, HD::HD1],
                    pZ[:, 0:8], OP.add))
                fr["ktm"] = t_sadd
                fr["pR0"] = t_sadd
                fr["pR1"] = t_sadd
                fr["pZ"] = t_sadd
                P.wait("tensor", t_scpy)
                P.wait("tensor", fr["pOA"])
                P.wait("tensor", fr["pOB"])
                P.wait("tensor", fr["pD"])
                for h in range(H):
                    po = pOA if h < 4 else pOB
                    P.em("tensor", lambda e, p=po, h_=h, i_=ic:
                         e.matmul(p[:, (h_ % 4) * 128:(h_ % 4) * 128 + 128],
                                  q_sb[:, h_ * T1 + i_ * 128:h_ * T1 + i_ * 128 + 128],
                                  state_sb[:, h_ * HD1:h_ * HD1 + HD],
                                  start=(h_ % 4 == 0), stop=False))
                    P.em("tensor", lambda e, h_=h, i_=ic:
                         e.matmul(pD[:, h_:h_ + 1],
                                  q_sb[:, h_ * T1 + i_ * 128:h_ * T1 + i_ * 128 + 128],
                                  state_sb[:, h_ * HD1 + HD:h_ * HD1 + HD1],
                                  start=(h_ == 0), stop=False))
                t_int = P.close("tensor", "pe")
                P.wait("tensor", m0)
                P.wait("tensor", m1)
                for h in range(H):
                    po = pOA if h < 4 else pOB
                    P.em("tensor", lambda e, p=po, h_=h, i_=ic:
                         e.matmul(p[:, (h_ % 4) * 128:(h_ % 4) * 128 + 128],
                                  sTm_sb[:, h_ * 128:h_ * 128 + 128],
                                  v_sb[:, i_ * D + h_ * 128:i_ * D + h_ * 128 + 128],
                                  start=False, stop=(h_ % 4 == 3)))
                for h in range(H):
                    P.em("tensor", lambda e, h_=h:
                         e.matmul(pD[:, h_:h_ + 1],
                                  sTm_sb[:, h_ * 128:h_ * 128 + 128],
                                  ones_c16[:], start=False, stop=(h_ == 7)))
                t_omm = P.close("tensor", "pe")
                fr["sTm"] = t_omm
                fr["state"] = t_int
                P.wait("scalar", t_omm)
                P.wait("scalar", fr["ostg"])
                oev0 = P.op("scalar", "ac", lambda e, j=ic: e.copy(
                    o_stage[:, j * D:j * D + 512], pOA[:]))
                oev1 = P.op("scalar", "ac", lambda e, j=ic: e.copy(
                    o_stage[:, j * D + 512:j * D + 1024], pOB[:]))
                dn_toks[ic] = P.op("scalar", "ac", lambda e, j=ic: e.copy(
                    dn_stage[:, 8 * j:8 * j + 8], pD[:, 0:8]))
                oev_toks[ic] = later(oev0, oev1)
                fr["pOA"] = oev_toks[ic]
                fr["pOB"] = oev_toks[ic]
                fr["pD"] = dn_toks[ic]
                jj = 4 * t + ic
                if jj < len(wmax_jobs):
                    emit_wmax_job(wmax_jobs[jj])
            while late_steps:
                late_steps.pop(0)()

            fr["ostg"] = None
            tail_steps, late_steps = make_tail(t, xb, oev_toks, dn_toks, msq_gate)

        while tail_steps:
            tail_steps.pop(0)()
        while late_steps:
            late_steps.pop(0)()

        for j in range(4 * NT1, len(wmax_jobs)):
            emit_wmax_job(wmax_jobs[j])
        if not scales_done[0]:
            emit_scales()

        t_scales = ("ve", P.counts["ve"])
        t_pe_end = ("pe", P.counts["pe"])
        t_ac_end = ("ac", P.counts["ac"])
        t_gd_end = ("gd", P.counts["gd"])
        pass1_end = later(t_pe_end, t_ac_end, t_scales, t_gd_end)

    with contextlib.ExitStack() as s2:
        sb2 = lambda name, shape, dt: s2.enter_context(nc.sbuf_tensor(name, shape, dt))
        WfcT = sb2("WfcT", [128, 8 * FF], BF16)
        WpjT = sb2("WpjT", [128, 32 * D], BF16)
        NW = 3
        wstg = [sb2(f"wstg{i}", [128, 1024], F32) for i in range(NW)]
        sgn_sb = [sb2(f"sgn_sb{i}", [128, 1024], mybir.dt.int8)
                  for i in range(2)]
        i16_sb = [sb2(f"i16_sb{i}", [128, 1024], mybir.dt.int8)
                  for i in range(2)]
        lora_sb = [sb2(f"lora_sb{i}", [128, 1024], BF16) for i in range(2)]
        x1_tm = [sb2(f"x1tm{i}", [128, 2 * D], F32) for i in range(2)]
        x1f16 = sb2("x1f16", [128, 2 * D], BF16)
        x1_fm = sb2("x1_fm", [128, 2 * D], BF16)
        h1_sb = [sb2(f"h1_{i}", [128, T2], BF16) for i in range(2)]
        r2_sb = [sb2(f"r2_{i}", [128, 2 * D], F32) for i in range(2)]
        out_sb = [sb2("out_0", [128, D], F32)] * 2

        ps2 = lambda name: s2.enter_context(nc.psum_tensor(name, [128, 512], F32))
        pA, pB = ps2("pA"), ps2("pB")
        pH = [ps2(f"pH{i}") for i in range(4)]
        pU1, pU2 = ps2("pU1"), ps2("pU2")

        P.wait("sync", pass1_end)

        x1_pre = {}
        for t0 in range(2):
            x1_pre[t0] = P.op("sync", "ld", lambda e, tt=t0, bb=t0 % 2: e.dma_start(
                out=x1_tm[bb][:].rearrange("p (c d) -> p c d", d=D),
                in_=x1s_d[tt * T2:(tt + 1) * T2, :]
                    .rearrange("(c p) d -> c p d", p=128).transpose([1, 0, 2])), 16)

        quant_jobs = []
        for qq in range(4):
            for dc in range(8):
                quant_jobs.append((
                    wfcT_d[dc * 128:(dc + 1) * 128, qq * 1024:(qq + 1) * 1024],
                    WfcT[:, dc * FF + qq * 1024:dc * FF + qq * 1024 + 1024],
                    sv_fc, si_fc, ("fcblk", qq),
                    afcN_d[:, dc * 128:(dc + 1) * 128], bfcB2_sb, qq * 1024))
            for k in range(8):
                fc = qq * 8 + k
                quant_jobs.append((
                    wprojT_d[fc * 128:(fc + 1) * 128, :],
                    WpjT[:, fc * D:(fc + 1) * D],
                    sv_pj, si_pj, ("pj", fc),
                    aprojN_d[:, fc * 128:(fc + 1) * 128], bprojB2_sb, 0))
        wfc_blk_ready = {}
        wpj_ready = {}
        fr2 = {"x1tm": [None, None], "x1f16": None, "x1fm": None,
               "pA": None, "pB": None, "h1": [None, None],
               "pH": None, "pU1": None, "pU2": None,
               "r2n": [None, None], "r2rd": [None, None], "out": None,
               "xl_pre": None, "c16_pre": None}
        qstate = {"j": 0, "sgn": [None, None], "stg": [None] * 3,
                  "aslc": [None, None], "a_tok": [None, None],
                  "lora": [None, None]}
        combine_toks = []
        t_lora_ready = later(t_bB1c, t_bB2c, t_bprow,
                             t_ms_toks[5], t_last_const)

        def emit_quant_job():
            j = qstate["j"]
            if j >= len(quant_jobs):
                return
            qstate["j"] = j + 1
            qsrc, qdst, sv, si, tag, asrc, lrhs, lcol = quant_jobs[j]
            b = j % 2
            w_ = j % NW
            if qstate["a_tok"][b] is None:
                t_a0 = P.op("gpsimd", "gd", lambda e, s=asrc, bb=b:
                            e.dma_start(out=aslc_sb[bb][:], in_=s), 16)
                qstate["a_tok"][b] = t_a0
            P.wait("tensor", t_lora_ready)
            P.wait("tensor", qstate["a_tok"][b])
            P.wait("tensor", fr2["pU1"])
            P.wait("tensor", fr2["pU2"])
            P.em("tensor", lambda e, bb=b, r=lrhs, co=lcol: e.matmul(
                pU1[:], aslc_sb[bb][:], r[0:LR, co:co + 512],
                start=True, stop=True))
            P.em("tensor", lambda e, bb=b, r=lrhs, co=lcol: e.matmul(
                pU2[:], aslc_sb[bb][:], r[0:LR, co + 512:co + 1024],
                start=True, stop=True))
            tk = P.close("tensor", "pe")
            qstate["aslc"][b] = tk
            P.wait("scalar", tk)
            P.wait("scalar", qstate["lora"][b])
            ev0 = P.op("scalar", "ac", lambda e, bb=b:
                       e.copy(lora_sb[bb][:, 0:512], pU1[:]))
            P.wait("vector", tk)
            P.wait("vector", qstate["lora"][b])
            ev1 = P.op("vector", "ve", lambda e, bb=b:
                       e.tensor_copy(lora_sb[bb][:, 512:1024], pU2[:]))
            fr2["pU1"] = ev0
            fr2["pU2"] = ev1
            if j + 1 < len(quant_jobs):
                b1 = (j + 1) % 2
                P.wait("gpsimd", qstate["aslc"][b1])
                qstate["a_tok"][b1] = P.op(
                    "gpsimd", "gd", lambda e, s=quant_jobs[j + 1][5], bb=b1:
                    e.dma_start(out=aslc_sb[bb][:], in_=s), 16)
            P.wait("sync", qstate["stg"][w_])
            t_l = P.op("sync", "ld", lambda e, s=qsrc, ww=w_:
                       e.dma_start(out=wstg[ww][:], in_=s), 16)
            P.wait("scalar", t_l)
            P.wait("scalar", qstate["sgn"][b])
            P.op("scalar", "ac", lambda e, bb=b, ww=w_:
                 e.sign(sgn_sb[bb][:], wstg[ww][:]))
            t_ab = P.op("scalar", "ac", lambda e, ww=w_, s=si: e.activation(
                wstg[ww][:], wstg[ww][:], AF.Abs, bias=0.0, scale=s[:]))
            t_ci = P.op("scalar", "ac", lambda e, bb=b, ww=w_:
                        e.copy(i16_sb[bb][:], wstg[ww][:]))
            P.wait("vector", t_ci)
            P.op("vector", "ve", lambda e, bb=b, ww=w_: e.scalar_tensor_tensor(
                wstg[ww][:], i16_sb[bb][:], -1.0, wstg[ww][:], OP.mult, OP.add))
            P.op("vector", "ve", lambda e, bb=b, ww=w_: e.scalar_tensor_tensor(
                wstg[ww][:], wstg[ww][:], 0.5, i16_sb[bb][:], OP.is_ge, OP.add))
            t_c = P.op("vector", "ve", lambda e, d=qdst, s=sv, bb=b, ww=w_:
                       e.scalar_tensor_tensor(d, wstg[ww][:], s[:], sgn_sb[bb][:],
                                              OP.mult, OP.mult))
            qstate["sgn"][b] = t_c
            qstate["stg"][w_] = t_c
            P.wait("vector", ev1)
            t_f = P.op("vector", "ve", lambda e, d=qdst, bb=b: e.tensor_tensor(
                d, lora_sb[bb][:], d, OP.add))
            qstate["lora"][b] = t_f
            combine_toks.append(t_f)
            if tag[0] == "fcblk":
                wfc_blk_ready[tag[1]] = t_f
            else:
                wpj_ready[tag[1]] = t_f

        pending_ln2 = []

        def make_ln2(t, b, res_t, sqs_out):
            st = {}

            def p_stats():
                st["red"] = P.op("vector", "ve", lambda e, bb=b: e.reduce_sum(
                    R1(8, 2)[:].rearrange("p (c o) -> p c o", o=1),
                    r2_sb[bb][:].rearrange("p (c d) -> p c d", d=D), axis=AXX))
                P.wait("scalar", res_t[-1])
                P.wait("scalar", res_t[-2])
                P.wait("scalar", fr2["out"])
                sqs = []
                for i2 in range(2):
                    sqs.append(P.op("scalar", "ac",
                               lambda e, ii=i2, bb=b: e.activation(
                                   out_sb[0][:],
                                   r2_sb[bb][:, ii * D:(ii + 1) * D], AF.Square,
                                   accum_out=R1(2 + ii)[:])))
                sqs_out.extend(sqs)
                st["sqs"] = sqs

            def p_rstd():
                P.wait("vector", st["red"])
                P.op("vector", "ve", lambda e: e.tensor_scalar_mul(
                    R1(10, 2)[:], R1(8, 2)[:], 1.0 / D))
                P.op("vector", "ve", lambda e: e.tensor_mul(
                    R1(12, 2)[:], R1(10, 2)[:], R1(10, 2)[:]))
                P.wait("vector", st["sqs"][-1])
                t_var = P.op("vector", "ve", lambda e: e.scalar_tensor_tensor(
                    R1(14, 2)[:], R1(2, 2)[:], 1.0 / D, R1(12, 2)[:],
                    OP.mult, OP.subtract))
                P.wait("scalar", t_var)
                t_sq = P.op("scalar", "ac", lambda e: e.activation(
                    R1(4, 2)[:], R1(14, 2)[:], AF.Sqrt, bias=eps_col[:],
                    scale=1.0))
                P.wait("vector", t_sq)
                P.op("vector", "ve", lambda e:
                     e.reciprocal(R1(4, 2)[:], R1(4, 2)[:]))

            def p_norm(i2):
                def f():
                    P.wait("vector", fr2["out"])
                    P.op("vector", "ve", lambda e, ii=i2, bb=b:
                         e.tensor_scalar_sub(
                             out_sb[ii][:], r2_sb[bb][:, ii * D:(ii + 1) * D],
                             R1(10 + ii)[:]))
                    tn = P.op("vector", "ve", lambda e, ii=i2:
                              e.tensor_scalar_mul(out_sb[ii][:], out_sb[ii][:],
                                                  R1(4 + ii)[:]))
                    st.setdefault("out_t", []).append(tn)
                    P.wait("gpsimd", tn)
                    fr2["out"] = P.op("gpsimd", "gd", lambda e, tt=t, ii=i2:
                        e.dma_start(
                            out=y_d[tt * T2 + ii * 128:tt * T2 + ii * 128 + 128, :],
                            in_=out_sb[ii][:]), 16)
                    if i2 == 1:
                        fr2["r2n"][b] = later(*st["out_t"])
                return f
            return [p_stats, p_rstd, p_norm(0), p_norm(1)]

        for t in range(NT2):
            b = t % 2
            if t in x1_pre:
                t_xl = x1_pre[t]
            else:
                t_xl = fr2["xl_pre"]
            if t == 0:
                P.wait("vector", t_xl)
                t_c16 = P.op("vector", "ve", lambda e:
                             e.tensor_copy(x1f16[:], x1_tm[0][:]))
            else:
                t_c16 = fr2["c16_pre"]
            P.wait("tensor", t_c16)
            P.wait("tensor", t_lora_ready)
            P.wait("tensor", fr2["x1fm"])
            P.wait("tensor", fr2["pA"])
            P.wait("tensor", fr2["pB"])
            P.wait("tensor", fr2["pU1"])
            P.wait("tensor", fr2["pU2"])
            banks4 = [pA, pB, pU1, pU2]
            for r in range(4):
                pq = banks4[r]
                for pp in range(4):
                    p_ = 4 * r + pp
                    dc, icb = p_ // 2, p_ % 2
                    P.em("tensor", lambda e, pq_=pq, pp_=pp, d_=dc, i_=icb:
                         e.matmul(pq_[:, pp_ * 128:pp_ * 128 + 128],
                                  x1f16[:, i_ * D + d_ * 128:i_ * D + d_ * 128 + 128],
                                  ident16[:], start=True, stop=True))
            tk = P.close("tensor", "pe")
            P.wait("scalar", tk)
            ev_t = {}
            for r in range(4):
                ev_t[r] = P.op("scalar", "ac", lambda e, pq_=banks4[r], r_=r:
                               e.copy(x1_fm[:, r_ * 512:r_ * 512 + 512], pq_[:]))
            fr2["x1f16"] = ("pe", P.counts["pe"])
            fr2["pA"], fr2["pB"] = ev_t[0], ev_t[1]
            fr2["pU1"], fr2["pU2"] = ev_t[2], ev_t[3]
            t_xfm = later(ev_t[0], ev_t[1], ev_t[2], ev_t[3])
            P.wait("tensor", fr2["pH"])
            h1_free = dict(fr2.get("h1f", {0: ev_t[0], 1: ev_t[1]}))
            gel = {0: None, 1: None}
            NFC = FF // 128

            def emit_h1(fc):
                hb = fc % 2
                pX = pA if hb == 0 else pB
                P.wait("tensor", wfc_blk_ready[fc // 8])
                P.wait("tensor", gel[hb])
                if fc > 0:
                    P.wait("tensor", t_xfm)
                for dc in range(8):
                    if fc == 0 and dc % 2 == 0:
                        P.wait("tensor", ev_t[dc // 2])
                    P.em("tensor", lambda e, p=pX, f_=fc, d_=dc:
                         e.matmul(p[:, 0:T2],
                                  WfcT[:, d_ * FF + f_ * 128:d_ * FF + f_ * 128 + 128],
                                  x1_fm[:, d_ * T2:(d_ + 1) * T2],
                                  start=(d_ == 0), stop=(d_ == 7)))
                t_h1 = P.close("tensor", "pe")
                P.wait("scalar", t_h1)
                P.wait("scalar", h1_free.get(hb))
                gel[hb] = P.op("scalar", "ac", lambda e, f_=fc, bb=hb: e.activation(
                    h1_sb[bb][:], (pA if bb == 0 else pB)[:, 0:T2],
                    AF.Gelu, bias=bfc_sb[:, f_:f_ + 1], scale=1.0))

            def emit_h2(fc):
                hb = fc % 2
                P.wait("tensor", wpj_ready[fc])
                P.wait("tensor", gel[hb])
                for i2 in range(2):
                    for half in range(2):
                        P.em("tensor", lambda e, f_=fc, bb=hb, ii=i2, hf=half:
                             e.matmul(pH[ii * 2 + hf][:],
                                      h1_sb[bb][:, ii * 128:ii * 128 + 128],
                                      WpjT[:, f_ * D + hf * 512:f_ * D + hf * 512 + 512],
                                      start=(f_ == 0), stop=False))
                h1_free[hb] = P.close("tensor", "pe")

            if t >= 2 and combine_toks:
                P.wait("tensor", later(*combine_toks))
            for fc in range(NFC):
                if fc in (4, 10, 16, 22) and pending_ln2:
                    pending_ln2.pop(0)()
                while (wfc_blk_ready.get(fc // 8) is None
                       or wpj_ready.get(fc) is None):
                    emit_quant_job()
                for _ in range(3):
                    emit_quant_job()
                if fc == 20 and t + 1 < NT2 and (t + 1) not in x1_pre:
                    bb1 = (t + 1) % 2
                    P.wait("sync", fr2["x1tm"][bb1])
                    fr2["xl_pre"] = P.op("sync", "ld", lambda e, tt=t + 1, bb=bb1:
                        e.dma_start(
                            out=x1_tm[bb][:].rearrange("p (c d) -> p c d", d=D),
                            in_=x1s_d[tt * T2:(tt + 1) * T2, :]
                            .rearrange("(c p) d -> c p d", p=128).transpose([1, 0, 2])), 16)
                if fc == 24 and t + 1 < NT2:
                    tl = x1_pre.get(t + 1, fr2.get("xl_pre"))
                    P.wait("vector", tl)
                    P.wait("vector", fr2["x1f16"])
                    fr2["c16_pre"] = P.op("vector", "ve",
                        lambda e, bb=(t + 1) % 2: e.tensor_copy(x1f16[:], x1_tm[bb][:]))
                emit_h1(fc)
                if fc > 0:
                    emit_h2(fc - 1)
            emit_h2(NFC - 1)
            fr2["h1f"] = h1_free
            fr2["pA"] = gel[0]
            fr2["pB"] = gel[1]
            for i2 in range(2):
                for half in range(2):
                    P.em("tensor", lambda e, ii=i2, hf=half:
                         e.matmul(pH[ii * 2 + hf][:], ones33[32:33, :],
                                  bprojB2_sb[32:33, hf * 512:hf * 512 + 512],
                                  start=False, stop=True))
            t_h2 = P.close("tensor", "pe")
            P.wait("vector", t_h2)
            P.wait("vector", fr2["r2n"][b])
            P.wait("vector", fr2["r2rd"][b])
            res_t = []
            for i2 in range(2):
                for half in range(2):
                    res_t.append(P.op("vector", "ve", lambda e, ii=i2, hf=half, bb=b:
                                 e.tensor_add(
                                     r2_sb[bb][:, ii * D + hf * 512:ii * D + hf * 512 + 512],
                                     pH[ii * 2 + hf][:],
                                     x1_tm[bb][:, ii * D + hf * 512:ii * D + hf * 512 + 512])))
            fr2["pH"] = later(*res_t)
            fr2["x1tm"][b] = res_t[-1]
            sqs_out = []
            fr2["r2rd"][b] = sqs_out
            pending_ln2.extend(make_ln2(t, b, res_t, sqs_out))
        while pending_ln2:
            pending_ln2.pop(0)()
        P.wait("gpsimd", ("gd", P.counts["gd"]))

    with nc.Block() as block:
        @block.sync
        def _(eng):
            for f in P.ops["sync"]:
                f(eng)

        @block.gpsimd
        def _(eng):
            for f in P.ops["gpsimd"]:
                f(eng)

        @block.tensor
        def _(eng):
            for f in P.ops["tensor"]:
                f(eng)

        @block.vector
        def _(eng):
            for f in P.ops["vector"]:
                f(eng)

        @block.scalar
        def _(eng):
            for f in P.ops["scalar"]:
                f(eng)

    ctx.close()
    return nc


def make_inputs(xb, wq, wk, wv, wo, nw, wfc, bfc, afc, bfcB, wproj, bproj,
                aproj, bprojB):
    f32 = np.float32
    com = {
        "wqT": np.ascontiguousarray(wq.T, dtype=f32),
        "wkT": np.ascontiguousarray(wk.T, dtype=f32),
        "wvT": np.ascontiguousarray(wv.T, dtype=f32),
        "woT": np.ascontiguousarray(wo.T, dtype=f32),
        "nw": np.ascontiguousarray(nw.reshape(HD, 1), dtype=f32),
        "wfcT": np.ascontiguousarray(wfc.T, dtype=f32),
        "bfc": np.ascontiguousarray(bfc.reshape(FF // 128, 128).T, dtype=f32),
        "afcN": np.ascontiguousarray(afc, dtype=f32),
        "bfcBT": np.ascontiguousarray(bfcB.T, dtype=f32),
        "wprojT": np.ascontiguousarray(wproj.T, dtype=f32),
        "bproj": np.ascontiguousarray(bproj.reshape(1, D), dtype=f32),
        "aprojN": np.ascontiguousarray(aproj, dtype=f32),
        "bprojBT": np.ascontiguousarray(bprojB.T, dtype=f32),
        "ident": np.eye(128, dtype=bfdt),
        "identf": np.eye(128, dtype=np.float32),
        "mask4": np.tile(np.triu(np.ones((128, 128), dtype=bfdt)), (1, 4)),
    }
    m = dict(com)
    m["x"] = np.ascontiguousarray(xb, dtype=f32)
    m["xT"] = np.ascontiguousarray(xb.T, dtype=f32)
    return m


def kernel(L0_attn_hidden_states, L0_attn_Wq, L0_attn_Wk, L0_attn_Wv,
           L0_attn_Wo, L0_attn_norm_w, L0_ffn_W_fc, L0_ffn_b_fc, L0_ffn_A_fc,
           L0_ffn_B_fc, L0_ffn_W_proj, L0_ffn_b_proj, L0_ffn_A_proj,
           L0_ffn_B_proj):
    x = np.asarray(L0_attn_hidden_states)
    B, S, _ = x.shape
    assert B == 8
    nc = build_program(S=S)
    in_maps = []
    for b in range(B):
        in_maps.append(make_inputs(
            x[b], np.asarray(L0_attn_Wq), np.asarray(L0_attn_Wk),
            np.asarray(L0_attn_Wv), np.asarray(L0_attn_Wo),
            np.asarray(L0_attn_norm_w), np.asarray(L0_ffn_W_fc),
            np.asarray(L0_ffn_b_fc), np.asarray(L0_ffn_A_fc),
            np.asarray(L0_ffn_B_fc), np.asarray(L0_ffn_W_proj),
            np.asarray(L0_ffn_b_proj), np.asarray(L0_ffn_A_proj),
            np.asarray(L0_ffn_B_proj)))
    res = run_bass_kernel_spmd(nc, in_maps, core_ids=list(range(B)))
    out = np.stack([np.asarray(r["y"]) for r in res.results], axis=0)
    return out


# version 32 (final)
# speedup vs baseline: 1.0043x; 1.0043x over previous
import contextlib
import numpy as np
import ml_dtypes

import concourse.bass as bass
import concourse.mybir as mybir
from concourse.bass_utils import run_bass_kernel_spmd
from concourse.alu_op_type import AluOpType as OP
from concourse.bass_isa import ReduceOp
from concourse import library_config

F32 = mybir.dt.float32
BF16 = mybir.dt.bfloat16
I32 = mybir.dt.int32
AF = mybir.ActivationFunctionType
AXX = mybir.AxisListType.X
bfdt = ml_dtypes.bfloat16

D = 1024
H = 8
HD = 128
HD1 = HD + 1
FF = 4096
LR = 16
EPS = 1e-5


class Prog:

    ENGINES = ("sync", "gpsimd", "tensor", "vector", "scalar")

    def __init__(self):
        self.ops = {e: [] for e in self.ENGINES}
        self.counts = {}
        self.sems = {}
        self.last_wait = {e: {} for e in self.ENGINES}

    def em(self, eng, fn):
        self.ops[eng].append(fn)

    def op(self, eng, sem, build_fn, n=1):
        self.wait(eng, (sem, self.counts.get(sem, 0)))
        self.counts[sem] = self.counts.get(sem, 0) + n
        v = self.counts[sem]

        def f(e, _b=build_fn, _s=sem, _n=n):
            ins = _b(e)
            ins.then_inc(self.sems[_s], _n)
            return ins
        f._is_op = True
        self.em(eng, f)
        return (sem, v)

    def close(self, eng, sem):
        last = self.ops[eng].pop()
        assert not getattr(last, "_is_wait", False), "close() popped a wait"
        assert not getattr(last, "_is_op", False), "close() popped a tokened op"
        return self.op(eng, sem, last)

    def wait(self, eng, tok):
        if tok is None:
            return
        if isinstance(tok, (list, tuple)) and tok and isinstance(tok[0], tuple):
            for t in tok:
                self.wait(eng, t)
            return
        sem, v = tok
        if self.last_wait[eng].get(sem, -1) >= v:
            return
        self.last_wait[eng][sem] = v

        def f(e, _s=sem, _v=v):
            e.wait_ge(self.sems[_s], _v)
        f._is_wait = True
        self.em(eng, f)


def later(*toks):
    out = {}
    for t in toks:
        if t is None:
            continue
        if isinstance(t, list):
            for tt in t:
                if tt is not None:
                    s, v = tt
                    out[s] = max(out.get(s, -1), v)
            continue
        s, v = t
        out[s] = max(out.get(s, -1), v)
    return [(s, v) for s, v in out.items()]


def build_program(S=8192, debug=False):
    nc = bass.Bass("TRN2", target_bir_lowering=False)
    T1 = 512
    NT1 = S // T1
    CPT = T1 // 128
    NCH = S // 128
    T2 = 256
    NT2 = S // T2

    x_tm_d = nc.dram_tensor("x", [S, D], F32, kind="ExternalInput").ap()
    xT_d = nc.dram_tensor("xT", [D, S], F32, kind="ExternalInput").ap()
    wqT_d = nc.dram_tensor("wqT", [D, D], F32, kind="ExternalInput").ap()
    wkT_d = nc.dram_tensor("wkT", [D, D], F32, kind="ExternalInput").ap()
    wvT_d = nc.dram_tensor("wvT", [D, D], F32, kind="ExternalInput").ap()
    woT_d = nc.dram_tensor("woT", [D, D], F32, kind="ExternalInput").ap()
    nw_d = nc.dram_tensor("nw", [HD, 1], F32, kind="ExternalInput").ap()
    wfcT_d = nc.dram_tensor("wfcT", [D, FF], F32, kind="ExternalInput").ap()
    bfc_d = nc.dram_tensor("bfc", [128, FF // 128], F32, kind="ExternalInput").ap()
    afcN_d = nc.dram_tensor("afcN", [LR, D], F32, kind="ExternalInput").ap()
    bfcBT_d = nc.dram_tensor("bfcBT", [LR, FF], F32, kind="ExternalInput").ap()
    wprojT_d = nc.dram_tensor("wprojT", [FF, D], F32, kind="ExternalInput").ap()
    bproj_d = nc.dram_tensor("bproj", [1, D], F32, kind="ExternalInput").ap()
    aprojN_d = nc.dram_tensor("aprojN", [LR, FF], F32, kind="ExternalInput").ap()
    bprojBT_d = nc.dram_tensor("bprojBT", [LR, D], F32, kind="ExternalInput").ap()
    ident_d = nc.dram_tensor("ident", [128, 128], BF16, kind="ExternalInput").ap()
    identf_d = nc.dram_tensor("identf", [128, 128], F32, kind="ExternalInput").ap()
    mask4_d = nc.dram_tensor("mask4", [128, 512], BF16, kind="ExternalInput").ap()

    y_d = nc.dram_tensor("y", [S, D], F32, kind="ExternalOutput").ap()
    x1s_d = nc.dram_tensor("x1s", [S, D], F32, kind="Internal").ap()
    if debug:
        x1dbg_d = nc.dram_tensor("x1dbg", [S, D], F32, kind="ExternalOutput").ap()

    P = Prog()
    ctx = contextlib.ExitStack()
    for s in ["ld", "gd", "pe", "ve", "ac", "gp"]:
        P.sems[s] = ctx.enter_context(nc.semaphore(s))
        P.counts[s] = 0

    sb = lambda name, shape, dt: ctx.enter_context(nc.sbuf_tensor(name, shape, dt))
    ident16 = sb("ident16", [128, 128], BF16)
    mask4 = sb("mask4_s", [128, 512], BF16)
    identf = sb("identf_s", [128, 128], F32)
    ones_r_f = sb("ones_r_f", [1, 128], F32)
    g11 = sb("g11", [1, 1], F32)
    ones_c16 = sb("ones_c16", [128, 1], BF16)
    nw_col = sb("nw_col", [128, 1], F32)
    eps_col = sb("eps_col", [128, 1], F32)
    bfc_sb = sb("bfc_sb", [128, FF // 128], F32)
    bfcB2_sb = sb("bfcB2_sb", [LR, FF], BF16)
    bprojB2_sb = sb("bprojB2_sb", [33, D], BF16)
    aslc_sb = [sb(f"aslc{i}", [LR, 128], BF16) for i in range(2)]
    ones33 = sb("ones33", [33, 128], BF16)
    wmax_fc = sb("wmax_fc", [128, 1], F32)
    wmax_pj = sb("wmax_pj", [128, 1], F32)
    sv_fc = sb("sv_fc", [128, 1], F32)
    si_fc = sb("si_fc", [128, 1], F32)
    sv_pj = sb("sv_pj", [128, 1], F32)
    si_pj = sb("si_pj", [128, 1], F32)
    rows1 = sb("rows1", [128, 32], F32)

    R1 = lambda i, w=1: rows1[:, i:i + w]

    lstg_ctx = contextlib.ExitStack()
    lstg = lstg_ctx.enter_context(nc.sbuf_tensor("lstg", [LR, FF], F32))
    for dst, src in [(ident16, ident_d), (identf, identf_d), (mask4, mask4_d),
                     (nw_col, nw_d), (bfc_sb, bfc_d)]:
        t_last_const = P.op("sync", "ld",
                            lambda e, d=dst, s=src: e.dma_start(out=d[:], in_=s), 16)
    t_nw = ("ld", 16 * 4)
    t_bB1 = P.op("sync", "ld", lambda e: e.dma_start(out=lstg[:], in_=bfcBT_d), 16)
    P.wait("vector", t_bB1)
    t_bB1c = P.op("vector", "ve",
                  lambda e: e.tensor_scalar_mul(bfcB2_sb[:], lstg[:], 2.0))
    P.wait("sync", t_bB1c)
    t_bB2 = P.op("sync", "ld",
                 lambda e: e.dma_start(out=lstg[0:LR, 0:D], in_=bprojBT_d), 16)
    P.wait("vector", t_bB2)
    t_bB2c = P.op("vector", "ve",
                  lambda e: e.tensor_scalar_mul(bprojB2_sb[0:LR, :], lstg[0:LR, 0:D], 2.0))
    t_bprow = P.op("gpsimd", "gd", lambda e: e.dma_start(
        out=bprojB2_sb[32:33, :], in_=bproj_d), 16)

    ms = lambda ap, v: P.op("vector", "ve", lambda e, a=ap, vv=v: e.memset(a, vv))
    t_ms_toks = [ms(ones_c16[:], 1.0), ms(ones_r_f[:], 1.0), ms(eps_col[:], EPS),
                 ms(wmax_fc[:], 0.0), ms(wmax_pj[:], 0.0),
                 ms(ones33[:], 1.0)]

    lstg_ctx.close()

    with contextlib.ExitStack() as s1:
        sb1 = lambda name, shape, dt: s1.enter_context(nc.sbuf_tensor(name, shape, dt))
        WqT = sb1("WqT", [128, 8 * D], BF16)
        WkT = sb1("WkT", [128, 8 * D], BF16)
        WvT = sb1("WvT", [128, 8 * D], BF16)
        WoT = sb1("WoT", [128, 8 * D], BF16)
        x_tm = [sb1(f"x_tm{i}", [128, CPT * D], F32) for i in range(2)]
        x_fm = sb1("x_fm0", [128, 8 * T1], BF16)
        q_sb = sb1("q_sb", [128, H * T1], BF16)
        k_sb = sb1("k_sb", [128, H * T1], BF16)
        v_sb = sb1("v_sb", [128, CPT * D], BF16)
        ktm_sb = sb1("ktm_sb", [128, D], BF16)
        sTm_sb = sb1("sTm_sb", [128, D], BF16)
        state_sb = sb1("state_sb", [128, H * HD1], BF16)
        state_f32 = sb1("state_f32", [128, H * HD1], F32)
        o_stage = sb1("o_stg", [128, CPT * D], BF16)
        dn_stage = sb1("dn_stg", [128, 32], F32)
        o2f_sb = sb1("o2f_sb", [128, D], BF16)
        osq_sb = [sb1(f"osq{i}", [128, D], BF16) for i in range(2)]
        r1_sb = [sb1(f"r1_{i}", [128, D], F32) for i in range(4)]
        x1o_sb = [sb1(f"x1o_{i}", [128, D], F32) for i in range(2)]
        wst_sb = sb1("wst_sb", [128, 1024], F32)
        rowsA = sb1("rowsA", [128, 128], F32)
        wtmp = sb1("wtmp", [128, 1], F32)
        RA = lambda i, w=8: rowsA[:, i:i + w]

        ps1 = lambda name: s1.enter_context(nc.psum_tensor(name, [128, 512], F32))
        pR0, pR1 = ps1("pR0"), ps1("pR1")
        pSA, pSB = ps1("pSA"), ps1("pSB")
        pZ = ps1("pZ")
        pOA, pOB = ps1("pOA"), ps1("pOB")
        pD = ps1("pD")

        P.wait("gpsimd", t_bB1c)
        P.wait("gpsimd", t_bB2c)
        w_toks = []
        for Wbuf, Wd in [(WqT, wqT_d), (WkT, wkT_d), (WvT, wvT_d), (WoT, woT_d)]:
            w_toks.append(P.op("gpsimd", "gd", lambda e, b=Wbuf, d=Wd: e.dma_start(
                out=b[:].rearrange("p (c d) -> p c d", d=D),
                in_=d.rearrange("(c p) d -> c p d", p=128).transpose([1, 0, 2])), 16))
        P.wait("vector", w_toks[3])
        P.wait("vector", t_nw)
        t_wofold = P.op("vector", "ve",
                        lambda e: e.tensor_scalar_mul(WoT[:], WoT[:], nw_col[:]))
        t_st0 = ms(state_sb[:], 0.0)
        t_stf0 = ms(state_f32[:], 0.0)

        fr = {"pR0": None, "pR1": None, "pOA": None, "pOB": None, "pD": None,
              "pSA": None, "pSB": None, "pZ": None, "ostg": None,
              "o2f": None, "sTm": None,
              "scales": None,
              "ktm": None, "state": None, "osq": [None, None],
              "xtm": [None, None], "r1": [None] * 4, "x1o": [None, None],
              "wst": None, "o1rd": [None, None]}
        t_upd = None
        t_consts_tok = ("ld", 16 * 5)
        t_all_w = later(*w_toks, t_wofold, t_consts_tok, *t_ms_toks)
        P.wait("vector", t_consts_tok)

        def emit_wmax_job(job):
            wsrc, inner, acc = job
            P.wait("sync", fr["wst"])
            t_ws = P.op("sync", "ld", lambda e, s=wsrc:
                        e.dma_start(out=wst_sb[:], in_=s), 16)
            P.wait("vector", t_ws)
            P.op("vector", "ve", lambda e: e.reduce_max(
                wtmp[:], wst_sb[:], axis=AXX, apply_absolute_value=True))
            fr["wst"] = P.op("vector", "ve",
                             lambda e, a=acc: e.tensor_max(a[:], a[:], wtmp[:]))

        wmax_jobs = []
        for i in range(8):
            for hf in range(4):
                wmax_jobs.append((wfcT_d[i * 128:(i + 1) * 128,
                                         hf * 1024:(hf + 1) * 1024], None, wmax_fc))
        for i in range(32):
            wmax_jobs.append((wprojT_d[i * 128:(i + 1) * 128, :], None, wmax_pj))

        qkv_rot = [0]
        QBANKS = [("pR0", pR0), ("pR1", pR1), ("pD", pD)]

        def qkv_groups(t):
            jobs = []
            for W, dst in [(WqT, q_sb), (WkT, k_sb)]:
                for h in range(H):
                    def job(W=W, dst=dst, h_=h):
                        nm, pq = QBANKS[qkv_rot[0] % 3]
                        qkv_rot[0] += 1
                        P.wait("tensor", fr[nm])
                        for dc in range(8):
                            P.em("tensor", lambda e, d_=dc:
                                 e.matmul(pq[:],
                                          W[:, d_ * D + h_ * HD:d_ * D + h_ * HD + HD],
                                          x_fm[:, d_ * T1:(d_ + 1) * T1],
                                          start=(d_ == 0), stop=(d_ == 7)))
                        tk = P.close("tensor", "pe")
                        eng, sem = (("vector", "ve") if qkv_rot[0] % 2 == 0
                                    else ("scalar", "ac"))
                        P.wait(eng, tk)
                        if eng == "vector":
                            fr[nm] = P.op(eng, sem, lambda e: e.tensor_copy(
                                dst[:, h_ * T1:(h_ + 1) * T1], pq[:]))
                        else:
                            fr[nm] = P.op(eng, sem, lambda e: e.copy(
                                dst[:, h_ * T1:(h_ + 1) * T1], pq[:]))
                    jobs.append(job)
            for ic in range(CPT):
                for half in range(2):
                    def job(i_=ic, hf=half):
                        nm, pq = QBANKS[qkv_rot[0] % 3]
                        qkv_rot[0] += 1
                        P.wait("tensor", fr[nm])
                        for dc in range(8):
                            P.em("tensor", lambda e, d_=dc:
                                 e.matmul(pq[:],
                                          x_fm[:, d_ * T1 + i_ * 128:d_ * T1 + i_ * 128 + 128],
                                          WvT[:, d_ * D + hf * 512:d_ * D + hf * 512 + 512],
                                          start=(d_ == 0), stop=(d_ == 7)))
                        tk = P.close("tensor", "pe")
                        eng, sem = (("vector", "ve") if qkv_rot[0] % 2 == 0
                                    else ("scalar", "ac"))
                        P.wait(eng, tk)
                        if eng == "vector":
                            fr[nm] = P.op(eng, sem, lambda e: e.tensor_copy(
                                v_sb[:, i_ * D + hf * 512:i_ * D + hf * 512 + 512], pq[:]))
                        else:
                            fr[nm] = P.op(eng, sem, lambda e: e.copy(
                                v_sb[:, i_ * D + hf * 512:i_ * D + hf * 512 + 512], pq[:]))
                    jobs.append(job)
            return jobs

        def make_tail(t, xb, oev_toks, dn_toks, msq_gate):
            steps = []

            def s_sq(j):
                def f():
                    P.wait("scalar", oev_toks[j])
                    P.wait("scalar", fr["osq"][j % 2])
                    t_sq_ = P.op("scalar", "ac", lambda e: e.activation(
                        osq_sb[j % 2][:], o_stage[:, j * D:(j + 1) * D], AF.Square))
                    P.wait("vector", t_sq_)
                    msq_gate[j] = P.op("vector", "ve", lambda e: e.reduce_sum(
                        RA(64 + 8 * j)[:].rearrange("p (h o) -> p h o", o=1),
                        osq_sb[j % 2][:].rearrange("p (h x) -> p h x", x=HD),
                        axis=AXX))
                    fr["osq"][j % 2] = msq_gate[j]
                return f
            for j in range(CPT):
                steps.append(s_sq(j))

            def s_dn():
                P.wait("scalar", later(*dn_toks))
                t_dabs = P.op("scalar", "ac", lambda e: e.activation(
                    RA(0, 32)[:], dn_stage[:], AF.Abs))
                P.wait("vector", t_dabs)
                P.op("vector", "ve", lambda e: e.tensor_scalar_max(
                    RA(0, 32)[:], RA(0, 32)[:], 1.0))
                P.op("vector", "ve", lambda e: e.tensor_mul(
                    RA(32, 32)[:], RA(0, 32)[:], RA(0, 32)[:]))
                P.wait("vector", later(*msq_gate))
                t_inner = P.op("vector", "ve", lambda e: e.scalar_tensor_tensor(
                    RA(96, 32)[:], RA(32, 32)[:], float(HD) * EPS,
                    RA(64, 32)[:], OP.mult, OP.add))
                P.wait("scalar", t_inner)
                t_s = P.op("scalar", "ac", lambda e: e.activation(
                    RA(96, 32)[:], RA(96, 32)[:], AF.Sqrt, bias=0.0, scale=1.0 / HD))
                P.wait("vector", t_s)
                fr["rtot"] = P.op("vector", "ve", lambda e: e.reciprocal(
                    RA(96, 32)[:], RA(96, 32)[:]))
            steps.append(s_dn)

            def s_apply(j):
                def f():
                    P.wait("vector", fr["rtot"])
                    P.wait("vector", fr["osq"][j % 2])
                    P.wait("vector", fr["o1rd"][j % 2])
                    ap = P.op("vector", "ve", lambda e: e.tensor_tensor(
                        osq_sb[j % 2][:].rearrange("p (c f) -> p c f", f=128),
                        o_stage[:, j * D:(j + 1) * D]
                        .rearrange("p (c f) -> p c f", f=128),
                        RA(96 + 8 * j)[:].rearrange("p (c o) -> p c o", o=1)
                        .broadcast_to([128, 8, 128]), OP.mult))
                    fr["apl", j] = ap
                return f

            def s_wo(j):
                def f():
                    P.wait("tensor", fr["apl", j])
                    P.wait("tensor", fr["o2f"])
                    P.wait("tensor", fr["pZ"])
                    for hh in range(4):
                        P.em("tensor", lambda e, h_=hh, c4=hh:
                             e.matmul(pZ[:, c4 * 128:c4 * 128 + 128],
                                      osq_sb[j % 2][:, h_ * 128:h_ * 128 + 128],
                                      ident16[:], start=True, stop=True))
                    tk0 = P.close("tensor", "pe")
                    P.wait("tensor", fr["pSA"])
                    for hh in range(4):
                        P.em("tensor", lambda e, h_=4 + hh, c4=hh:
                             e.matmul(pSA[:, c4 * 128:c4 * 128 + 128],
                                      osq_sb[j % 2][:, h_ * 128:h_ * 128 + 128],
                                      ident16[:], start=True, stop=True))
                    tk1 = P.close("tensor", "pe")
                    P.wait("scalar", tk0)
                    fr["pZ"] = P.op("scalar", "ac", lambda e: e.copy(
                        o2f_sb[:, 0:512], pZ[:]))
                    P.wait("vector", tk1)
                    fr["pSA"] = P.op("vector", "ve", lambda e: e.tensor_copy(
                        o2f_sb[:, 512:1024], pSA[:]))
                    fr["o1rd"][j % 2] = ("pe", P.counts["pe"])
                    P.wait("tensor", fr["pZ"])
                    P.wait("tensor", fr["pOA"])
                    P.wait("tensor", fr["pOB"])
                    for half, po in [(0, pOA), (1, pOB)]:
                        for h in range(4):
                            P.em("tensor", lambda e, p=po, h_=h, hf=half:
                                 e.matmul(p[:], o2f_sb[:, h_ * 128:h_ * 128 + 128],
                                          WoT[:, h_ * D + hf * 512:h_ * D + hf * 512 + 512],
                                          start=(h_ == 0), stop=False))
                    P.wait("tensor", fr["pSA"])
                    for half, po in [(0, pOA), (1, pOB)]:
                        for h in range(4, H):
                            P.em("tensor", lambda e, p=po, h_=h, hf=half:
                                 e.matmul(p[:], o2f_sb[:, h_ * 128:h_ * 128 + 128],
                                          WoT[:, h_ * D + hf * 512:h_ * D + hf * 512 + 512],
                                          start=False, stop=(h_ == 7)))
                    t_wo = P.close("tensor", "pe")
                    fr["o2f"] = t_wo
                    P.wait("vector", t_wo)
                    P.wait("vector", fr["r1"][j])
                    ra = P.op("vector", "ve", lambda e: e.scalar_tensor_tensor(
                        r1_sb[j][:, 0:512], pOA[:], 1.0,
                        x_tm[xb][:, j * D:j * D + 512],
                        OP.mult, OP.add, accum_out=R1(2 * j)[:]))
                    rbt = P.op("vector", "ve", lambda e: e.scalar_tensor_tensor(
                        r1_sb[j][:, 512:1024], pOB[:], 1.0,
                        x_tm[xb][:, j * D + 512:j * D + 1024],
                        OP.mult, OP.add, accum_out=R1(2 * j + 1)[:]))
                    fr["pOA"] = later(ra)
                    fr["pOB"] = later(rbt)
                    fr["res", j] = later(ra, rbt)
                    if j == CPT - 1:
                        fr["xtm"][xb] = rbt
                return f
            for j in range(CPT):
                steps.append(s_apply(j))
                steps.append(s_wo(j))

            def s_ssq(j):
                def f():
                    P.wait("scalar", fr["res", j])
                    P.wait("scalar", fr["apl", j])
                    P.wait("scalar", fr["ostg2", j] if ("ostg2", j) in fr else None)
                    tk = P.op("scalar", "ac", lambda e: e.activation(
                        o_stage[:, j * D:(j + 1) * D], r1_sb[j][:], AF.Square,
                        accum_out=R1(28 + j)[:]))
                    fr["ssq", j] = tk
                    fr["ostg2", j] = tk
                return f
            for j in range(CPT):
                steps.append(s_ssq(j))

            def s_ln():
                P.wait("vector", later(*[fr["ssq", j] for j in range(CPT)]))
                P.op("vector", "ve", lambda e: e.tensor_add(
                    R1(8, 4)[:], R1(0, 8)[:].rearrange("p (c two) -> p c two", two=2)
                    [:, :, 0:1].squeeze(2) if False else R1(0, 8)[:]
                    .rearrange("p (c two) -> p c two", two=2)[:, :, 0],
                    R1(0, 8)[:].rearrange("p (c two) -> p c two", two=2)[:, :, 1]))
                P.op("vector", "ve", lambda e: e.tensor_scalar_mul(
                    R1(8, 4)[:], R1(8, 4)[:], 1.0 / D))
                P.op("vector", "ve", lambda e: e.tensor_mul(
                    R1(12, 4)[:], R1(8, 4)[:], R1(8, 4)[:]))
                t_var = P.op("vector", "ve", lambda e: e.scalar_tensor_tensor(
                    R1(16, 4)[:], R1(28, 4)[:], 1.0 / D, R1(12, 4)[:],
                    OP.mult, OP.subtract))
                P.wait("scalar", t_var)
                t_sq = P.op("scalar", "ac", lambda e: e.activation(
                    R1(20, 4)[:], R1(16, 4)[:], AF.Sqrt, bias=eps_col[:], scale=1.0))
                P.wait("vector", t_sq)
                P.op("vector", "ve", lambda e: e.reciprocal(R1(20, 4)[:], R1(20, 4)[:]))
                fr["b1"] = P.op("vector", "ve", lambda e: e.scalar_tensor_tensor(
                    R1(24, 4)[:], R1(8, 4)[:], -1.0, R1(20, 4)[:],
                    OP.mult, OP.mult))
            late = [s_ln]

            def s_out(j):
                def f():
                    c = t * CPT + j
                    P.wait("scalar", fr["b1"])
                    P.wait("scalar", fr["x1o"][j % 2])
                    t_x1 = P.op("scalar", "ac", lambda e: e.activation(
                        x1o_sb[j % 2][:], r1_sb[j][:], AF.Identity,
                        bias=R1(24 + j)[:], scale=R1(20 + j)[:]))
                    fr["r1"][j] = t_x1
                    P.wait("gpsimd", t_x1)
                    t_st = P.op("gpsimd", "gd", lambda e: e.dma_start(
                        out=x1s_d[c * 128:(c + 1) * 128, :], in_=x1o_sb[j % 2][:]), 16)
                    if debug:
                        t_st = P.op("gpsimd", "gd", lambda e: e.dma_start(
                            out=x1dbg_d[c * 128:(c + 1) * 128, :],
                            in_=x1o_sb[j % 2][:]), 16)
                    fr["x1o"][j % 2] = t_st
                return f
            for j in range(CPT):
                late.append(s_out(j))
            return steps, late

        scales_done = [False]

        def emit_scales():
            scales_done[0] = True
            for wmax, sv, si in [(wmax_fc, sv_fc, si_fc), (wmax_pj, sv_pj, si_pj)]:
                P.wait("tensor", fr["wst"])
                P.wait("tensor", later(*t_ms_toks))
                P.wait("tensor", fr["pR0"])
                P.em("tensor", lambda e, w=wmax: e.matmul(
                    pR0[0:1, 0:128], w[:], identf[:], start=True, stop=True))
                tk = P.close("tensor", "pe")
                P.wait("vector", tk)
                t_g = P.op("vector", "ve", lambda e: e.reduce_max(
                    g11[:], pR0[0:1, 0:128], axis=AXX))
                P.wait("tensor", t_g)
                P.em("tensor", lambda e: e.matmul(pR0[:, 0:1], ones_r_f[:], g11[:],
                                                  start=True, stop=True))
                tk = P.close("tensor", "pe")
                P.wait("vector", tk)
                P.op("vector", "ve", lambda e, s=sv: e.tensor_scalar_mul(
                    s[:], pR0[:, 0:1], 1.0 / 127.0))
                P.op("vector", "ve", lambda e, s=si: e.reciprocal(s[:], pR0[:, 0:1]))
                t_si = P.op("vector", "ve", lambda e, s=si: e.tensor_scalar_mul(
                    s[:], s[:], 127.0))
                fr["pR0"] = t_si
                fr["wst"] = t_si
            fr["scales"] = ("ve", P.counts["ve"])

        tail_steps = []
        late_steps = []
        t_xf = None
        for t in range(NT1):
            xb = t % 2
            P.wait("sync", fr["xtm"][xb])
            t_x = P.op("sync", "ld", lambda e, tt=t, b=xb: e.dma_start(
                out=x_tm[b][:].rearrange("p (c d) -> p c d", d=D),
                in_=x_tm_d[tt * T1:(tt + 1) * T1, :]
                    .rearrange("(c p) d -> c p d", p=128).transpose([1, 0, 2])), 16)
            if t == 0:
                t_xf = P.op("gpsimd", "gd", lambda e: e.dma_start(
                    out=x_fm[:].rearrange("p (c i) -> p c i", i=T1),
                    in_=xT_d[:, 0:T1]
                        .rearrange("(c p) i -> c p i", p=128).transpose([1, 0, 2])), 16)
                P.wait("tensor", t_all_w)

            P.wait("tensor", t_xf)
            for gi, grp in enumerate(qkv_groups(t)):
                grp()
                if gi >= 4 and tail_steps:
                    tail_steps.pop(0)()
            while tail_steps:
                tail_steps.pop(0)()
            if t + 1 < NT1:
                P.wait("gpsimd", ("pe", P.counts["pe"]))
                t_xf = P.op("gpsimd", "gd", lambda e, tt=t + 1: e.dma_start(
                    out=x_fm[:].rearrange("p (c i) -> p c i", i=T1),
                    in_=xT_d[:, tt * T1:(tt + 1) * T1]
                        .rearrange("(c p) i -> c p i", p=128).transpose([1, 0, 2])), 16)

            oev_toks = [None] * CPT
            dn_toks = [None] * CPT
            msq_gate = [None] * CPT
            for ic in range(CPT):
                c = t * CPT + ic
                P.wait("tensor", fr["pR0"])
                P.wait("tensor", fr["pR1"])
                P.wait("tensor", fr["ktm"])
                for h in range(H):
                    pq = pR0 if h < 4 else pR1
                    P.em("tensor", lambda e, p=pq, h_=h, i_=ic:
                         e.matmul(p[:, (h_ % 4) * 128:(h_ % 4) * 128 + 128],
                                  k_sb[:, h_ * T1 + i_ * 128:h_ * T1 + i_ * 128 + 128],
                                  ident16[:], start=True, stop=True))
                t_ktm = P.close("tensor", "pe")
                P.wait("tensor", fr["pSA"])
                P.wait("tensor", fr["pSB"])
                for h in range(H):
                    pq = pSA if h < 4 else pSB
                    P.em("tensor", lambda e, p=pq, h_=h, i_=ic:
                         e.matmul(p[:, (h_ % 4) * 128:(h_ % 4) * 128 + 128],
                                  k_sb[:, h_ * T1 + i_ * 128:h_ * T1 + i_ * 128 + 128],
                                  q_sb[:, h_ * T1 + i_ * 128:h_ * T1 + i_ * 128 + 128],
                                  start=True, stop=True))
                t_sc = P.close("tensor", "pe")
                P.wait("scalar", t_ktm)
                ev0 = P.op("scalar", "ac", lambda e:
                           e.copy(ktm_sb[:, 0:512], pR0[:]))
                ev1 = P.op("scalar", "ac", lambda e:
                           e.copy(ktm_sb[:, 512:1024], pR1[:]))
                P.wait("vector", t_sc)
                P.wait("vector", fr["sTm"])
                m0 = P.op("vector", "ve", lambda e: e.tensor_tensor(
                    sTm_sb[:, 0:512], pSA[:], mask4[:], OP.mult))
                m1 = P.op("vector", "ve", lambda e: e.tensor_tensor(
                    sTm_sb[:, 512:1024], pSB[:], mask4[:], OP.mult))
                fr["pSA"], fr["pSB"] = m0, m1
                if late_steps:
                    late_steps.pop(0)()
                if c > 0:
                    P.wait("vector", fr["state"])
                    t_scpy = P.op("vector", "ve", lambda e: e.tensor_copy(
                        state_sb[:], state_f32[:]))
                else:
                    t_scpy = later(t_st0, t_stf0)
                P.wait("tensor", ev0)
                P.wait("tensor", ev1)
                P.wait("tensor", fr["pZ"])
                for h in range(H):
                    psb = pR0 if h < 4 else pR1
                    P.em("tensor", lambda e, p=psb, h_=h, i_=ic:
                         e.matmul(p[:, (h_ % 4) * 128:(h_ % 4) * 128 + 128],
                                  ktm_sb[:, h_ * 128:h_ * 128 + 128],
                                  v_sb[:, i_ * D + h_ * 128:i_ * D + h_ * 128 + 128],
                                  start=(h_ % 4 == 0), stop=(h_ % 4 == 3)))
                for h in range(H):
                    P.em("tensor", lambda e, h_=h:
                         e.matmul(pZ[:, h_:h_ + 1],
                                  ktm_sb[:, h_ * 128:h_ * 128 + 128], ones_c16[:],
                                  start=(h_ == 0), stop=(h_ == 7)))
                t_upd = P.close("tensor", "pe")
                P.wait("vector", t_upd)
                P.op("vector", "ve", lambda e: e.tensor_tensor(
                    state_f32[:].rearrange("p (h x) -> p h x", x=HD1)[:, 0:4, 0:HD],
                    state_f32[:].rearrange("p (h x) -> p h x", x=HD1)[:, 0:4, 0:HD],
                    pR0[:].rearrange("p (h x) -> p h x", x=HD), OP.add))
                P.op("vector", "ve", lambda e: e.tensor_tensor(
                    state_f32[:].rearrange("p (h x) -> p h x", x=HD1)[:, 4:8, 0:HD],
                    state_f32[:].rearrange("p (h x) -> p h x", x=HD1)[:, 4:8, 0:HD],
                    pR1[:].rearrange("p (h x) -> p h x", x=HD), OP.add))
                t_sadd = P.op("vector", "ve", lambda e: e.tensor_tensor(
                    state_f32[:, HD::HD1], state_f32[:, HD::HD1],
                    pZ[:, 0:8], OP.add))
                fr["ktm"] = t_sadd
                fr["pR0"] = t_sadd
                fr["pR1"] = t_sadd
                fr["pZ"] = t_sadd
                P.wait("tensor", t_scpy)
                P.wait("tensor", fr["pOA"])
                P.wait("tensor", fr["pOB"])
                P.wait("tensor", fr["pD"])
                for h in range(H):
                    po = pOA if h < 4 else pOB
                    P.em("tensor", lambda e, p=po, h_=h, i_=ic:
                         e.matmul(p[:, (h_ % 4) * 128:(h_ % 4) * 128 + 128],
                                  q_sb[:, h_ * T1 + i_ * 128:h_ * T1 + i_ * 128 + 128],
                                  state_sb[:, h_ * HD1:h_ * HD1 + HD],
                                  start=(h_ % 4 == 0), stop=False))
                    P.em("tensor", lambda e, h_=h, i_=ic:
                         e.matmul(pD[:, h_:h_ + 1],
                                  q_sb[:, h_ * T1 + i_ * 128:h_ * T1 + i_ * 128 + 128],
                                  state_sb[:, h_ * HD1 + HD:h_ * HD1 + HD1],
                                  start=(h_ == 0), stop=False))
                t_int = P.close("tensor", "pe")
                P.wait("tensor", m0)
                P.wait("tensor", m1)
                for h in range(H):
                    po = pOA if h < 4 else pOB
                    P.em("tensor", lambda e, p=po, h_=h, i_=ic:
                         e.matmul(p[:, (h_ % 4) * 128:(h_ % 4) * 128 + 128],
                                  sTm_sb[:, h_ * 128:h_ * 128 + 128],
                                  v_sb[:, i_ * D + h_ * 128:i_ * D + h_ * 128 + 128],
                                  start=False, stop=(h_ % 4 == 3)))
                for h in range(H):
                    P.em("tensor", lambda e, h_=h:
                         e.matmul(pD[:, h_:h_ + 1],
                                  sTm_sb[:, h_ * 128:h_ * 128 + 128],
                                  ones_c16[:], start=False, stop=(h_ == 7)))
                t_omm = P.close("tensor", "pe")
                fr["sTm"] = t_omm
                fr["state"] = t_int
                P.wait("scalar", t_omm)
                P.wait("scalar", fr["ostg"])
                oev0 = P.op("scalar", "ac", lambda e, j=ic: e.copy(
                    o_stage[:, j * D:j * D + 512], pOA[:]))
                oev1 = P.op("scalar", "ac", lambda e, j=ic: e.copy(
                    o_stage[:, j * D + 512:j * D + 1024], pOB[:]))
                dn_toks[ic] = P.op("scalar", "ac", lambda e, j=ic: e.copy(
                    dn_stage[:, 8 * j:8 * j + 8], pD[:, 0:8]))
                oev_toks[ic] = later(oev0, oev1)
                fr["pOA"] = oev_toks[ic]
                fr["pOB"] = oev_toks[ic]
                fr["pD"] = dn_toks[ic]
                jj = 4 * t + ic
                if jj < len(wmax_jobs):
                    emit_wmax_job(wmax_jobs[jj])
            while late_steps:
                late_steps.pop(0)()

            fr["ostg"] = None
            tail_steps, late_steps = make_tail(t, xb, oev_toks, dn_toks, msq_gate)

        while tail_steps:
            tail_steps.pop(0)()
        while late_steps:
            late_steps.pop(0)()

        for j in range(4 * NT1, len(wmax_jobs)):
            emit_wmax_job(wmax_jobs[j])
        if not scales_done[0]:
            emit_scales()

        t_scales = ("ve", P.counts["ve"])
        t_pe_end = ("pe", P.counts["pe"])
        t_ac_end = ("ac", P.counts["ac"])
        t_gd_end = ("gd", P.counts["gd"])
        pass1_end = later(t_pe_end, t_ac_end, t_scales, t_gd_end)

    with contextlib.ExitStack() as s2:
        sb2 = lambda name, shape, dt: s2.enter_context(nc.sbuf_tensor(name, shape, dt))
        WfcT = sb2("WfcT", [128, 8 * FF], BF16)
        WpjT = sb2("WpjT", [128, 32 * D], BF16)
        NW = 3
        wstg = [sb2(f"wstg{i}", [128, 1024], F32) for i in range(NW)]
        sgn_sb = [sb2(f"sgn_sb{i}", [128, 1024], mybir.dt.int8)
                  for i in range(2)]
        i16_sb = [sb2(f"i16_sb{i}", [128, 1024], mybir.dt.int8)
                  for i in range(2)]
        lora_sb = [sb2(f"lora_sb{i}", [128, 1024], BF16) for i in range(2)]
        x1_tm = [sb2(f"x1tm{i}", [128, 2 * D], F32) for i in range(2)]
        x1f16 = sb2("x1f16", [128, 2 * D], BF16)
        x1_fm = sb2("x1_fm", [128, 2 * D], BF16)
        h1_sb = [sb2(f"h1_{i}", [128, T2], BF16) for i in range(2)]
        r2_sb = [sb2(f"r2_{i}", [128, 2 * D], F32) for i in range(2)]
        out_sb = [sb2("out_0", [128, D], F32)] * 2

        ps2 = lambda name: s2.enter_context(nc.psum_tensor(name, [128, 512], F32))
        pA, pB = ps2("pA"), ps2("pB")
        pH = [ps2(f"pH{i}") for i in range(4)]
        pU1, pU2 = ps2("pU1"), ps2("pU2")

        P.wait("sync", pass1_end)

        x1_pre = {}
        for t0 in range(2):
            x1_pre[t0] = P.op("sync", "ld", lambda e, tt=t0, bb=t0 % 2: e.dma_start(
                out=x1_tm[bb][:].rearrange("p (c d) -> p c d", d=D),
                in_=x1s_d[tt * T2:(tt + 1) * T2, :]
                    .rearrange("(c p) d -> c p d", p=128).transpose([1, 0, 2])), 16)

        quant_jobs = []
        for qq in range(4):
            for dc in range(8):
                quant_jobs.append((
                    wfcT_d[dc * 128:(dc + 1) * 128, qq * 1024:(qq + 1) * 1024],
                    WfcT[:, dc * FF + qq * 1024:dc * FF + qq * 1024 + 1024],
                    sv_fc, si_fc, ("fcblk", qq),
                    afcN_d[:, dc * 128:(dc + 1) * 128], bfcB2_sb, qq * 1024))
            for k in range(8):
                fc = qq * 8 + k
                quant_jobs.append((
                    wprojT_d[fc * 128:(fc + 1) * 128, :],
                    WpjT[:, fc * D:(fc + 1) * D],
                    sv_pj, si_pj, ("pj", fc),
                    aprojN_d[:, fc * 128:(fc + 1) * 128], bprojB2_sb, 0))
        wfc_blk_ready = {}
        wpj_ready = {}
        fr2 = {"x1tm": [None, None], "x1f16": None, "x1fm": None,
               "pA": None, "pB": None, "h1": [None, None],
               "pH": None, "pU1": None, "pU2": None,
               "r2n": [None, None], "r2rd": [None, None], "out": None,
               "xl_pre": None, "c16_pre": None}
        qstate = {"j": 0, "sgn": [None, None], "stg": [None] * 3,
                  "aslc": [None, None], "a_tok": [None, None],
                  "lora": [None, None]}
        combine_toks = []
        t_lora_ready = later(t_bB1c, t_bB2c, t_bprow,
                             t_ms_toks[5], t_last_const)

        def emit_quant_job():
            j = qstate["j"]
            if j >= len(quant_jobs):
                return
            qstate["j"] = j + 1
            qsrc, qdst, sv, si, tag, asrc, lrhs, lcol = quant_jobs[j]
            b = j % 2
            w_ = j % NW
            if qstate["a_tok"][b] is None:
                t_a0 = P.op("gpsimd", "gd", lambda e, s=asrc, bb=b:
                            e.dma_start(out=aslc_sb[bb][:], in_=s), 16)
                qstate["a_tok"][b] = t_a0
            P.wait("tensor", t_lora_ready)
            P.wait("tensor", qstate["a_tok"][b])
            P.wait("tensor", fr2["pU1"])
            P.wait("tensor", fr2["pU2"])
            P.em("tensor", lambda e, bb=b, r=lrhs, co=lcol: e.matmul(
                pU1[:], aslc_sb[bb][:], r[0:LR, co:co + 512],
                start=True, stop=True))
            P.em("tensor", lambda e, bb=b, r=lrhs, co=lcol: e.matmul(
                pU2[:], aslc_sb[bb][:], r[0:LR, co + 512:co + 1024],
                start=True, stop=True))
            tk = P.close("tensor", "pe")
            qstate["aslc"][b] = tk
            P.wait("scalar", tk)
            P.wait("scalar", qstate["lora"][b])
            ev0 = P.op("scalar", "ac", lambda e, bb=b:
                       e.copy(lora_sb[bb][:, 0:512], pU1[:]))
            P.wait("vector", tk)
            P.wait("vector", qstate["lora"][b])
            ev1 = P.op("vector", "ve", lambda e, bb=b:
                       e.tensor_copy(lora_sb[bb][:, 512:1024], pU2[:]))
            fr2["pU1"] = ev0
            fr2["pU2"] = ev1
            if j + 1 < len(quant_jobs):
                b1 = (j + 1) % 2
                P.wait("gpsimd", qstate["aslc"][b1])
                qstate["a_tok"][b1] = P.op(
                    "gpsimd", "gd", lambda e, s=quant_jobs[j + 1][5], bb=b1:
                    e.dma_start(out=aslc_sb[bb][:], in_=s), 16)
            P.wait("sync", qstate["stg"][w_])
            t_l = P.op("sync", "ld", lambda e, s=qsrc, ww=w_:
                       e.dma_start(out=wstg[ww][:], in_=s), 16)
            P.wait("scalar", t_l)
            P.wait("scalar", qstate["sgn"][b])
            P.op("scalar", "ac", lambda e, bb=b, ww=w_:
                 e.sign(sgn_sb[bb][:], wstg[ww][:]))
            t_ab = P.op("scalar", "ac", lambda e, ww=w_, s=si: e.activation(
                wstg[ww][:], wstg[ww][:], AF.Abs, bias=0.0, scale=s[:]))
            t_ci = P.op("scalar", "ac", lambda e, bb=b, ww=w_:
                        e.copy(i16_sb[bb][:], wstg[ww][:]))
            P.wait("vector", t_ci)
            P.op("vector", "ve", lambda e, bb=b, ww=w_: e.scalar_tensor_tensor(
                wstg[ww][:], i16_sb[bb][:], -1.0, wstg[ww][:], OP.mult, OP.add))
            P.op("vector", "ve", lambda e, bb=b, ww=w_: e.scalar_tensor_tensor(
                wstg[ww][:], wstg[ww][:], 0.5, i16_sb[bb][:], OP.is_ge, OP.add))
            t_c = P.op("vector", "ve", lambda e, d=qdst, s=sv, bb=b, ww=w_:
                       e.scalar_tensor_tensor(d, wstg[ww][:], s[:], sgn_sb[bb][:],
                                              OP.mult, OP.mult))
            qstate["sgn"][b] = t_c
            qstate["stg"][w_] = t_c
            P.wait("vector", ev1)
            t_f = P.op("vector", "ve", lambda e, d=qdst, bb=b: e.tensor_tensor(
                d, lora_sb[bb][:], d, OP.add))
            qstate["lora"][b] = t_f
            combine_toks.append(t_f)
            if tag[0] == "fcblk":
                wfc_blk_ready[tag[1]] = t_f
            else:
                wpj_ready[tag[1]] = t_f

        pending_ln2 = []

        def make_ln2(t, b, res_t, sqs_out):
            st = {}

            def p_stats():
                st["red"] = P.op("vector", "ve", lambda e, bb=b: e.reduce_sum(
                    R1(8, 2)[:].rearrange("p (c o) -> p c o", o=1),
                    r2_sb[bb][:].rearrange("p (c d) -> p c d", d=D), axis=AXX))
                P.wait("scalar", res_t[-1])
                P.wait("scalar", res_t[-2])
                P.wait("scalar", fr2["out"])
                sqs = []
                for i2 in range(2):
                    sqs.append(P.op("scalar", "ac",
                               lambda e, ii=i2, bb=b: e.activation(
                                   out_sb[0][:],
                                   r2_sb[bb][:, ii * D:(ii + 1) * D], AF.Square,
                                   accum_out=R1(2 + ii)[:])))
                sqs_out.extend(sqs)
                st["sqs"] = sqs

            def p_rstd():
                P.wait("vector", st["red"])
                P.op("vector", "ve", lambda e: e.tensor_scalar_mul(
                    R1(10, 2)[:], R1(8, 2)[:], 1.0 / D))
                P.op("vector", "ve", lambda e: e.tensor_mul(
                    R1(12, 2)[:], R1(10, 2)[:], R1(10, 2)[:]))
                P.wait("vector", st["sqs"][-1])
                t_var = P.op("vector", "ve", lambda e: e.scalar_tensor_tensor(
                    R1(14, 2)[:], R1(2, 2)[:], 1.0 / D, R1(12, 2)[:],
                    OP.mult, OP.subtract))
                P.wait("scalar", t_var)
                t_sq = P.op("scalar", "ac", lambda e: e.activation(
                    R1(4, 2)[:], R1(14, 2)[:], AF.Sqrt, bias=eps_col[:],
                    scale=1.0))
                P.wait("vector", t_sq)
                P.op("vector", "ve", lambda e:
                     e.reciprocal(R1(4, 2)[:], R1(4, 2)[:]))

            def p_norm(i2):
                def f():
                    tb = P.op("vector", "ve", lambda e, ii=i2:
                              e.scalar_tensor_tensor(
                                  R1(6 + ii)[:], R1(10 + ii)[:], -1.0,
                                  R1(4 + ii)[:], OP.mult, OP.mult))
                    P.wait("scalar", tb)
                    P.wait("scalar", fr2["out"])
                    tn = P.op("scalar", "ac", lambda e, ii=i2, bb=b:
                              e.activation(
                                  out_sb[ii][:],
                                  r2_sb[bb][:, ii * D:(ii + 1) * D], AF.Identity,
                                  bias=R1(6 + ii)[:], scale=R1(4 + ii)[:]))
                    st.setdefault("out_t", []).append(tn)
                    P.wait("gpsimd", tn)
                    fr2["out"] = P.op("gpsimd", "gd", lambda e, tt=t, ii=i2:
                        e.dma_start(
                            out=y_d[tt * T2 + ii * 128:tt * T2 + ii * 128 + 128, :],
                            in_=out_sb[ii][:]), 16)
                    if i2 == 1:
                        fr2["r2n"][b] = later(*st["out_t"])
                return f
            return [p_stats, p_rstd, p_norm(0), p_norm(1)]

        for t in range(NT2):
            b = t % 2
            if t in x1_pre:
                t_xl = x1_pre[t]
            else:
                t_xl = fr2["xl_pre"]
            if t == 0:
                P.wait("vector", t_xl)
                t_c16 = P.op("vector", "ve", lambda e:
                             e.tensor_copy(x1f16[:], x1_tm[0][:]))
            else:
                t_c16 = fr2["c16_pre"]
            P.wait("tensor", t_c16)
            P.wait("tensor", t_lora_ready)
            P.wait("tensor", fr2["x1fm"])
            P.wait("tensor", fr2["pA"])
            P.wait("tensor", fr2["pB"])
            P.wait("tensor", fr2["pU1"])
            P.wait("tensor", fr2["pU2"])
            banks4 = [pA, pB, pU1, pU2]
            for r in range(4):
                pq = banks4[r]
                for pp in range(4):
                    p_ = 4 * r + pp
                    dc, icb = p_ // 2, p_ % 2
                    P.em("tensor", lambda e, pq_=pq, pp_=pp, d_=dc, i_=icb:
                         e.matmul(pq_[:, pp_ * 128:pp_ * 128 + 128],
                                  x1f16[:, i_ * D + d_ * 128:i_ * D + d_ * 128 + 128],
                                  ident16[:], start=True, stop=True))
            tk = P.close("tensor", "pe")
            P.wait("scalar", tk)
            ev_t = {}
            for r in range(4):
                ev_t[r] = P.op("scalar", "ac", lambda e, pq_=banks4[r], r_=r:
                               e.copy(x1_fm[:, r_ * 512:r_ * 512 + 512], pq_[:]))
            fr2["x1f16"] = ("pe", P.counts["pe"])
            fr2["pA"], fr2["pB"] = ev_t[0], ev_t[1]
            fr2["pU1"], fr2["pU2"] = ev_t[2], ev_t[3]
            t_xfm = later(ev_t[0], ev_t[1], ev_t[2], ev_t[3])
            P.wait("tensor", fr2["pH"])
            h1_free = dict(fr2.get("h1f", {0: ev_t[0], 1: ev_t[1]}))
            gel = {0: None, 1: None}
            NFC = FF // 128

            def emit_h1(fc):
                hb = fc % 2
                pX = pA if hb == 0 else pB
                P.wait("tensor", wfc_blk_ready[fc // 8])
                P.wait("tensor", gel[hb])
                if fc > 0:
                    P.wait("tensor", t_xfm)
                for dc in range(8):
                    if fc == 0 and dc % 2 == 0:
                        P.wait("tensor", ev_t[dc // 2])
                    P.em("tensor", lambda e, p=pX, f_=fc, d_=dc:
                         e.matmul(p[:, 0:T2],
                                  WfcT[:, d_ * FF + f_ * 128:d_ * FF + f_ * 128 + 128],
                                  x1_fm[:, d_ * T2:(d_ + 1) * T2],
                                  start=(d_ == 0), stop=(d_ == 7)))
                t_h1 = P.close("tensor", "pe")
                P.wait("scalar", t_h1)
                P.wait("scalar", h1_free.get(hb))
                gel[hb] = P.op("scalar", "ac", lambda e, f_=fc, bb=hb: e.activation(
                    h1_sb[bb][:], (pA if bb == 0 else pB)[:, 0:T2],
                    AF.Gelu, bias=bfc_sb[:, f_:f_ + 1], scale=1.0))

            def emit_h2(fc):
                hb = fc % 2
                P.wait("tensor", wpj_ready[fc])
                P.wait("tensor", gel[hb])
                for i2 in range(2):
                    for half in range(2):
                        P.em("tensor", lambda e, f_=fc, bb=hb, ii=i2, hf=half:
                             e.matmul(pH[ii * 2 + hf][:],
                                      h1_sb[bb][:, ii * 128:ii * 128 + 128],
                                      WpjT[:, f_ * D + hf * 512:f_ * D + hf * 512 + 512],
                                      start=(f_ == 0), stop=False))
                h1_free[hb] = P.close("tensor", "pe")

            if t >= 2 and combine_toks:
                P.wait("tensor", later(*combine_toks))
            for fc in range(NFC):
                if fc in (4, 10, 16, 22) and pending_ln2:
                    pending_ln2.pop(0)()
                while (wfc_blk_ready.get(fc // 8) is None
                       or wpj_ready.get(fc) is None):
                    emit_quant_job()
                for _ in range(3):
                    emit_quant_job()
                if fc == 20 and t + 1 < NT2 and (t + 1) not in x1_pre:
                    bb1 = (t + 1) % 2
                    P.wait("sync", fr2["x1tm"][bb1])
                    fr2["xl_pre"] = P.op("sync", "ld", lambda e, tt=t + 1, bb=bb1:
                        e.dma_start(
                            out=x1_tm[bb][:].rearrange("p (c d) -> p c d", d=D),
                            in_=x1s_d[tt * T2:(tt + 1) * T2, :]
                            .rearrange("(c p) d -> c p d", p=128).transpose([1, 0, 2])), 16)
                if fc == 24 and t + 1 < NT2:
                    tl = x1_pre.get(t + 1, fr2.get("xl_pre"))
                    P.wait("vector", tl)
                    P.wait("vector", fr2["x1f16"])
                    fr2["c16_pre"] = P.op("vector", "ve",
                        lambda e, bb=(t + 1) % 2: e.tensor_copy(x1f16[:], x1_tm[bb][:]))
                emit_h1(fc)
                if fc > 0:
                    emit_h2(fc - 1)
            emit_h2(NFC - 1)
            fr2["h1f"] = h1_free
            fr2["pA"] = gel[0]
            fr2["pB"] = gel[1]
            for i2 in range(2):
                for half in range(2):
                    P.em("tensor", lambda e, ii=i2, hf=half:
                         e.matmul(pH[ii * 2 + hf][:], ones33[32:33, :],
                                  bprojB2_sb[32:33, hf * 512:hf * 512 + 512],
                                  start=False, stop=True))
            t_h2 = P.close("tensor", "pe")
            P.wait("vector", t_h2)
            P.wait("vector", fr2["r2n"][b])
            P.wait("vector", fr2["r2rd"][b])
            res_t = []
            for i2 in range(2):
                for half in range(2):
                    res_t.append(P.op("vector", "ve", lambda e, ii=i2, hf=half, bb=b:
                                 e.tensor_add(
                                     r2_sb[bb][:, ii * D + hf * 512:ii * D + hf * 512 + 512],
                                     pH[ii * 2 + hf][:],
                                     x1_tm[bb][:, ii * D + hf * 512:ii * D + hf * 512 + 512])))
            fr2["pH"] = later(*res_t)
            fr2["x1tm"][b] = res_t[-1]
            sqs_out = []
            fr2["r2rd"][b] = sqs_out
            pending_ln2.extend(make_ln2(t, b, res_t, sqs_out))
        while pending_ln2:
            pending_ln2.pop(0)()
        P.wait("gpsimd", ("gd", P.counts["gd"]))

    with nc.Block() as block:
        @block.sync
        def _(eng):
            for f in P.ops["sync"]:
                f(eng)

        @block.gpsimd
        def _(eng):
            for f in P.ops["gpsimd"]:
                f(eng)

        @block.tensor
        def _(eng):
            for f in P.ops["tensor"]:
                f(eng)

        @block.vector
        def _(eng):
            for f in P.ops["vector"]:
                f(eng)

        @block.scalar
        def _(eng):
            for f in P.ops["scalar"]:
                f(eng)

    ctx.close()
    return nc


def make_inputs(xb, wq, wk, wv, wo, nw, wfc, bfc, afc, bfcB, wproj, bproj,
                aproj, bprojB):
    f32 = np.float32
    com = {
        "wqT": np.ascontiguousarray(wq.T, dtype=f32),
        "wkT": np.ascontiguousarray(wk.T, dtype=f32),
        "wvT": np.ascontiguousarray(wv.T, dtype=f32),
        "woT": np.ascontiguousarray(wo.T, dtype=f32),
        "nw": np.ascontiguousarray(nw.reshape(HD, 1), dtype=f32),
        "wfcT": np.ascontiguousarray(wfc.T, dtype=f32),
        "bfc": np.ascontiguousarray(bfc.reshape(FF // 128, 128).T, dtype=f32),
        "afcN": np.ascontiguousarray(afc, dtype=f32),
        "bfcBT": np.ascontiguousarray(bfcB.T, dtype=f32),
        "wprojT": np.ascontiguousarray(wproj.T, dtype=f32),
        "bproj": np.ascontiguousarray(bproj.reshape(1, D), dtype=f32),
        "aprojN": np.ascontiguousarray(aproj, dtype=f32),
        "bprojBT": np.ascontiguousarray(bprojB.T, dtype=f32),
        "ident": np.eye(128, dtype=bfdt),
        "identf": np.eye(128, dtype=np.float32),
        "mask4": np.tile(np.triu(np.ones((128, 128), dtype=bfdt)), (1, 4)),
    }
    m = dict(com)
    m["x"] = np.ascontiguousarray(xb, dtype=f32)
    m["xT"] = np.ascontiguousarray(xb.T, dtype=f32)
    return m


def kernel(L0_attn_hidden_states, L0_attn_Wq, L0_attn_Wk, L0_attn_Wv,
           L0_attn_Wo, L0_attn_norm_w, L0_ffn_W_fc, L0_ffn_b_fc, L0_ffn_A_fc,
           L0_ffn_B_fc, L0_ffn_W_proj, L0_ffn_b_proj, L0_ffn_A_proj,
           L0_ffn_B_proj):
    x = np.asarray(L0_attn_hidden_states)
    B, S, _ = x.shape
    assert B == 8
    nc = build_program(S=S)
    in_maps = []
    for b in range(B):
        in_maps.append(make_inputs(
            x[b], np.asarray(L0_attn_Wq), np.asarray(L0_attn_Wk),
            np.asarray(L0_attn_Wv), np.asarray(L0_attn_Wo),
            np.asarray(L0_attn_norm_w), np.asarray(L0_ffn_W_fc),
            np.asarray(L0_ffn_b_fc), np.asarray(L0_ffn_A_fc),
            np.asarray(L0_ffn_B_fc), np.asarray(L0_ffn_W_proj),
            np.asarray(L0_ffn_b_proj), np.asarray(L0_ffn_A_proj),
            np.asarray(L0_ffn_B_proj)))
    res = run_bass_kernel_spmd(nc, in_maps, core_ids=list(range(B)))
    out = np.stack([np.asarray(r["y"]) for r in res.results], axis=0)
    return out
